# version 23
# baseline (speedup 1.0000x reference)
import numpy as np
import ml_dtypes
import concourse.bass as bass
import concourse.mybir as mybir
from concourse.bass_utils import run_bass_kernel_spmd
from contextlib import ExitStack

F32 = mybir.dt.float32
I32 = mybir.dt.int32
BF16 = mybir.dt.bfloat16
AF = mybir.ActivationFunctionType
ALU = mybir.AluOpType
AX = mybir.AxisListType


def full_cfg():
    return dict(H=2048, I=1024, E=16, TOPK=4, GT=8, NG=2,
                PAIRS=[[0, 1], [2, 3], [4, 5], [6, 7]])


def derived(cfg):
    d = dict(cfg)
    d["KH"] = cfg["H"] // 128
    d["KI"] = cfg["I"] // 128
    d["TG"] = cfg["GT"] * 128
    d["TILES"] = cfg["GT"] * cfg["NG"]
    d["TLOC"] = d["TILES"] * 128
    d["NSLOT"] = 4
    return d



def pack_weights(cfg, wg, wu, wd, sw_g, sw_u, sw_d, W_in, W_gate, gate_w):
    H, I, E, KH, KI = cfg["H"], cfg["I"], cfg["E"], cfg["KH"], cfg["KI"]
    MI = KI
    wg_all = np.concatenate([wg, sw_g[None]], 0)
    wu_all = np.concatenate([wu, sw_u[None]], 0)
    wd_all = np.concatenate([wd, sw_d[None]], 0)

    def gu(a):
        return a.reshape(E + 1, MI, 128, KH, 128).transpose(0, 1, 4, 3, 2)

    arr_gu = np.ascontiguousarray(
        np.stack([gu(wg_all), gu(wu_all)], axis=2))
    arr_wd = np.ascontiguousarray(
        wd_all.reshape(E + 1, H, KI, 128).transpose(0, 2, 3, 1))

    def io(W):
        return W.reshape(KH, 128, KH, 128).transpose(0, 3, 2, 1)

    arr_io = np.ascontiguousarray(
        np.stack([io(W_in), io(W_gate)], axis=1))
    arr_gw = np.ascontiguousarray(gate_w.T.reshape(KH, 128, E))
    return arr_gu, arr_wd, arr_io, arr_gw



ENGS = ("tensor", "vector", "scalar", "gpsimd", "sync")


class Sem:
    __slots__ = ("h", "n", "name")

    def __init__(self, h, name):
        self.h = h
        self.n = 0
        self.name = name


class Plan:

    def __init__(self):
        self.ops = {e: [] for e in ENGS}
        self.last_w = {}
        self.readers = {}
        self.hw = {e: {} for e in ENGS}
        self.region_of = {}
        self.region_keys = {}

    def _cands(self, key):
        fam = key[0]
        out = [key]
        reg = self.region_of.get(fam)
        if reg is not None:
            for f2, ks in self.region_keys.get(reg, {}).items():
                if f2 != fam:
                    out.extend(ks)
        return out

    def _register(self, key):
        fam = key[0]
        reg = self.region_of.get(fam)
        if reg is not None:
            self.region_keys.setdefault(reg, {}).setdefault(fam, set()).add(key)

    def step(self, eng, reads=(), writes=(), sem=None, inc=0, f=None):
        need = {}

        def add(marks):
            for (s, v) in marks:
                if v > need.get(s, 0):
                    need[s] = v
        for k in reads:
            for c in self._cands(k):
                if c in self.last_w:
                    add([self.last_w[c]])
        for k in writes:
            for c in self._cands(k):
                if c in self.last_w:
                    add([self.last_w[c]])
                if c in self.readers:
                    add(self.readers[c])
        hw = self.hw[eng]
        waits = []
        for s, v in sorted(need.items(), key=lambda kv: kv[0].name):
            if hw.get(s.name, 0) < v:
                waits.append((s, v))
                hw[s.name] = v
        if sem is not None:
            sem.n += inc
            marker = (sem, sem.n)
        else:
            marker = None
        self.ops[eng].append((waits, f, sem, inc))
        if marker is not None:
            for k in writes:
                self._register(k)
                self.last_w[k] = marker
                self.readers[k] = []
            for k in reads:
                self._register(k)
                self.readers.setdefault(k, []).append(marker)
        else:
            assert not writes
        return marker

    def join_keys(self, keys, sem):
        for k in keys:
            if k in self.last_w and self.last_w[k][0] is sem:
                self.last_w[k] = (sem, sem.n)

    def wait_abs(self, eng, sem, val):
        hw = self.hw[eng]
        if hw.get(sem.name, 0) < val:
            hw[sem.name] = val
            self.ops[eng].append(([(sem, val)], None, None, 0))

    def emit(self, block):
        def runner(lst):
            def run(engine):
                for waits, f, sem, inc in lst:
                    for (s, v) in waits:
                        engine.wait_ge(s.h, v)
                    if f is not None:
                        instr = f(engine)
                        if sem is not None:
                            instr.then_inc(sem.h, inc)
            return run
        block.sync(runner(self.ops["sync"]))
        block.tensor(runner(self.ops["tensor"]))
        block.vector(runner(self.ops["vector"]))
        block.scalar(runner(self.ops["scalar"]))
        block.gpsimd(runner(self.ops["gpsimd"]))



def build_program(cfg, detect_races=False):
    cfg = derived(cfg)
    H, I, E, KH, KI = cfg["H"], cfg["I"], cfg["E"], cfg["KH"], cfg["KI"]
    MI, GT, NG, TG = KI, cfg["GT"], cfg["NG"], cfg["TG"]
    TILES, TLOC, NSLOT = cfg["TILES"], cfg["TLOC"], cfg["NSLOT"]
    TOPK = cfg["TOPK"]
    TCH = [(s, min(512, TG - s)) for s in range(0, TG, 512)]
    HCH = [(s, min(512, H - s)) for s in range(0, H, 512)]
    EPS_LN = 1e-5

    nc = bass.Bass("TRN2", target_bir_lowering=False, debug=False,
                   detect_race_conditions=detect_races)
    arena = nc.alloc_sbuf_tensor("arena0", [128, 206 * 1024], mybir.dt.uint8)
    ABASE = nc.lookup_mloc(arena).addr

    x_in = nc.dram_tensor("x", [TLOC, H], F32, kind="ExternalInput").ap()
    wgu = nc.dram_tensor("wgu", [E + 1, MI, 2, 128, KH, 128], F32, kind="ExternalInput").ap()
    wdd = nc.dram_tensor("wd", [E + 1, KI, 128, H], F32, kind="ExternalInput").ap()
    wio = nc.dram_tensor("wio", [KH, 2, 128, KH, 128], F32, kind="ExternalInput").ap()
    gw = nc.dram_tensor("gw", [KH, 128, E], F32, kind="ExternalInput").ap()
    ebias = nc.dram_tensor("ebias", [128, E], F32, kind="ExternalInput").ap()
    bin_d = nc.dram_tensor("bin", [128, KH], F32, kind="ExternalInput").ap()
    bgate_d = nc.dram_tensor("bgate", [128, KH], F32, kind="ExternalInput").ap()
    lam_d = nc.dram_tensor("lam", [128, KH], F32, kind="ExternalInput").ap()
    flag_d = nc.dram_tensor("flag", [128, 1], F32, kind="ExternalInput").ap()
    nflag_d = nc.dram_tensor("nflag", [128, 1], F32, kind="ExternalInput").ap()
    ident_d = nc.dram_tensor("identf", [128, 128], F32, kind="ExternalInput").ap()
    y_out = nc.dram_tensor("y", [TLOC, H], F32, kind="ExternalOutput").ap()

    xr = nc.dram_tensor("xr", [TLOC, H], F32).ap()
    cgu = nc.dram_tensor("cgu", [E + 1, MI, 2, 128, KH, 128], BF16).ap()
    cwd = nc.dram_tensor("cwd", [E + 1, KI, 128, H], BF16).ap()
    cio = nc.dram_tensor("cio", [KH, 2, 128, KH, 128], BF16).ap()
    hdram = nc.dram_tensor("hdram", [KH, 128, TLOC], F32).ap()
    adram = nc.dram_tensor("adram", [KH, 128, TLOC], F32).ap()
    shd = nc.dram_tensor("shd", [2, 128, KH], F32, addr_space="Shared").ap()
    shf = nc.dram_tensor("shf", [2, 1, 1], I32, addr_space="Shared").ap()
    onec_d = nc.dram_tensor("onec", [128, 1], F32, kind="ExternalInput").ap()
    nonce_d = nc.dram_tensor("nonce", [1, 1], I32, kind="ExternalInput").ap()
    half_d = nc.dram_tensor("halfi", [1, 1], I32, kind="ExternalInput").ap()

    st = ExitStack()
    EE = st.enter_context

    off = [0]

    def alloc(name, shape, dtype, at=None):
        per = int(np.prod(shape[1:])) * mybir.dt.size(dtype)
        if at is None:
            at = off[0]
            off[0] = at + ((per + 127) // 128) * 128
        t = nc.alloc_sbuf_tensor_at(name, list(shape), dtype, offset=ABASE + at)
        return t, at

    def region(n_bytes):
        at = off[0]
        off[0] = at + ((n_bytes + 127) // 128) * 128
        return at

    xg, _ = alloc("xg", [128, GT, H], F32)
    xlnT, _ = alloc("xlnT", [128, KH, TG], BF16)
    r_wd = region(max(KI * H * 2, 8 * TG * 4, 4 * H * 4))
    wdsb, _ = alloc("wdsb", [128, KI, H], BF16, at=r_wd)
    gfa, _ = alloc("gfa", [128, 8, TG], F32, at=r_wd)
    ydmp, _ = alloc("ydmp", [128, 4, H], F32, at=r_wd)
    r_hp = region(max(MI * TG * 2, 4 * TG * 4))
    hp, _ = alloc("hp", [128, MI, TG], BF16, at=r_hp)
    gfb, _ = alloc("gfb", [128, 4, TG], F32, at=r_hp)
    zb, _ = alloc("zb", [128, TG], F32, at=r_hp)
    gur, _ = alloc("gur", [128, NSLOT, KH, 128], BF16)
    stg, _ = alloc("stg", [128, 2, H], F32)
    r_yf = region(max(2 * H * 4, 4 * TG * 4))
    yf, _ = alloc("yf", [128, 2, H], F32, at=r_yf)
    wdstg, _ = alloc("wdstg", [128, 2, H], F32, at=r_yf)
    ght, _ = alloc("ght", [128, 2, TG], F32, at=r_yf)
    gsc, _ = alloc("gsc", [128, 4, TG], F32, at=r_yf)
    sg, _ = alloc("sg", [128, 2, TG], F32)
    cbuf, _ = alloc("cbuf", [128, GT, E], F32)
    stats, _ = alloc("stats", [128, GT, 8], F32)
    rtr, _ = alloc("rtr", [128, 4, max(E, 4)], F32)
    rst, _ = alloc("rst", [128, 4], F32)
    gwb, _ = alloc("gwb", [128, KH, E], BF16)
    gwf, _ = alloc("gwf", [128, KH, E], F32)
    ebs, _ = alloc("ebs", [128, E], F32)
    binb, _ = alloc("binb", [128, KH], F32)
    bgb, _ = alloc("bgb", [128, KH], F32)
    lamb, _ = alloc("lamb", [128, KH], F32)
    sdm, _ = alloc("sdm", [128, KH], F32)
    sdmh, _ = alloc("sdmh", [128, KH], F32)
    bgbh, _ = alloc("bgbh", [128, KH], F32)
    binbh, _ = alloc("binbh", [128, KH], F32)
    tA, _ = alloc("tA", [128, KH], F32)
    tB, _ = alloc("tB", [128, KH], F32)
    flg, _ = alloc("flg", [128, 1], F32)
    onec, _ = alloc("onec", [128, 1], F32)
    nflg, _ = alloc("nflg", [128, 1], F32)
    idf, _ = alloc("idf", [128, 128], F32)
    hend, _ = alloc("hend", [128, KH], F32)
    aend, _ = alloc("aend", [128, KH], F32)
    ceff, _ = alloc("ceff", [128, KH], F32)
    ccrs, _ = alloc("ccrs", [128, KH], F32)
    nsb, _ = alloc("nsb", [1, 4], I32)
    hsb, _ = alloc("hsb", [1, 4], I32)
    fsb, _ = alloc("fsb", [1, 4], I32)
    assert off[0] <= 206 * 1024, f"sbuf over budget: {off[0]/1024:.1f} KiB"

    ps_a = EE(nc.psum_tensor("ps_a", [128, 1024], F32))
    ps_b = EE(nc.psum_tensor("ps_b", [128, 1024], F32))
    ps_c = EE(nc.psum_tensor("ps_c", [128, 1024], F32))
    ps_d = EE(nc.psum_tensor("ps_d", [128, 1024], F32))
    PS = {"a": ps_a, "b": ps_b, "c": ps_c, "d": ps_d}

    p = Plan()
    p.region_of.update({
        "wd": "R_wd", "gfa": "R_wd", "ydmp": "R_wd",
        "hp": "R_hp", "gfb": "R_hp", "zb": "R_hp",
        "yf": "R_yf", "ght": "R_yf", "gsc": "R_yf", "wdstg": "R_yf",
        "sg": "R_sg", "sgh": "R_sg",
    })

    def mksem(name):
        return Sem(EE(nc.semaphore(name)), name)

    sT, sV, sS, sG = mksem("sT"), mksem("sV"), mksem("sS"), mksem("sG")
    dCst = mksem("dCst")
    dX = [mksem(f"dX{t}") for t in range(GT)]
    dXW = [mksem(f"dXW{t}") for t in range(GT)]
    dGu = [mksem(f"dGu{s}") for s in range(NSLOT)]
    dCw = [mksem(f"dCw{s}") for s in range(NSLOT)]
    dStg = [mksem(f"dStg{s}") for s in range(4)]
    dStgW = [mksem(f"dStgW{s}") for s in range(2)]
    dWd = [mksem(f"dWd{k}") for k in range(KI)]
    dCwd = [mksem(f"dCwd{k}") for k in range(KI)]
    dHdH = [mksem(f"dHdH{r}") for r in range(2)]
    dHdA = [mksem(f"dHdA{r}") for r in range(2)]
    dHb = [mksem(f"dHb{s}") for s in range(2)]
    dAb = [mksem(f"dAb{s}") for s in range(2)]
    dCc = mksem("dCc")
    dPoll = EE(nc.semaphore("dPoll"))

    def dma(eng, sem, out_ap, in_ap, reads=(), writes=()):
        return p.step(eng, reads=reads, writes=writes, sem=sem, inc=16,
                      f=lambda e, o=out_ap, i=in_ap: e.dma_start(out=o, in_=i))

    const_keys = [("gwf", 0), ("ebs", 0), ("binb", 0), ("bgb", 0), ("lamb", 0),
                  ("flg", 0), ("nflg", 0), ("idf", 0), ("onec", 0)]
    dma("sync", dCst, gwf[:, :, :], gw.rearrange("k p e -> p k e"), writes=[("gwf", 0)])
    dma("sync", dCst, ebs[:, :], ebias, writes=[("ebs", 0)])
    dma("sync", dCst, binb[:, :], bin_d, writes=[("binb", 0)])
    dma("sync", dCst, bgb[:, :], bgate_d, writes=[("bgb", 0)])
    dma("sync", dCst, lamb[:, :], lam_d, writes=[("lamb", 0)])
    dma("sync", dCst, flg[:, :], flag_d, writes=[("flg", 0)])
    dma("sync", dCst, onec[:, :], onec_d, writes=[("onec", 0)])
    dma("sync", dCst, nflg[:, :], nflag_d, writes=[("nflg", 0)])
    dma("sync", dCst, idf[:, :], ident_d, writes=[("idf", 0)])
    dma("sync", dCst, nsb[:, 0:1], nonce_d[0], writes=[("nsb", 0)])
    dma("sync", dCst, hsb[:, 0:1], half_d[0], writes=[("hsb", 0)])
    const_keys.extend([("nsb", 0), ("hsb", 0)])
    p.join_keys(const_keys, dCst)

    p.step("vector", reads=[("gwf", 0)], writes=[("gwb", 0)], sem=sV, inc=1,
           f=lambda e: e.tensor_copy(out=gwb[:, :, :], in_=gwf[:, :, :]))
    p.step("scalar", reads=[("lamb", 0)], writes=[("tA", 0)], sem=sS, inc=1,
           f=lambda e: e.activation(out=tA[:, :], in_=lamb[:, :], func=AF.Abs))
    p.step("scalar", reads=[("tA", 0)], writes=[("tA", 0)], sem=sS, inc=1,
           f=lambda e: e.activation(out=tA[:, :], in_=tA[:, :], func=AF.Exp, scale=-1.0))
    p.step("vector", reads=[("tA", 0)], writes=[("tA", 0)], sem=sV, inc=1,
           f=lambda e: e.tensor_scalar_add(out=tA[:, :], in0=tA[:, :], scalar1=1.0))
    p.step("scalar", reads=[("tA", 0)], writes=[("tA", 0)], sem=sS, inc=1,
           f=lambda e: e.activation(out=tA[:, :], in_=tA[:, :], func=AF.Ln))
    p.step("vector", reads=[("lamb", 0)], writes=[("tB", 0)], sem=sV, inc=1,
           f=lambda e: e.tensor_scalar_max(out=tB[:, :], in0=lamb[:, :], scalar1=0.0))
    p.step("vector", reads=[("tA", 0), ("tB", 0)], writes=[("tB", 0)], sem=sV, inc=1,
           f=lambda e: e.tensor_add(out=tB[:, :], in0=tB[:, :], in1=tA[:, :]))
    p.step("vector", reads=[("tB", 0)], writes=[("sdm", 0)], sem=sV, inc=1,
           f=lambda e: e.tensor_scalar_mul(out=sdm[:, :], in0=tB[:, :], scalar1=8.0))
    p.step("vector", reads=[("tB", 0)], writes=[("sdmh", 0)], sem=sV, inc=1,
           f=lambda e: e.tensor_scalar_mul(out=sdmh[:, :], in0=tB[:, :], scalar1=4.0))
    p.step("vector", reads=[("bgb", 0)], writes=[("bgbh", 0)], sem=sV, inc=1,
           f=lambda e: e.tensor_scalar_mul(out=bgbh[:, :], in0=bgb[:, :], scalar1=0.5))
    p.step("vector", reads=[("binb", 0)], writes=[("binbh", 0)], sem=sV, inc=1,
           f=lambda e: e.tensor_scalar_mul(out=binbh[:, :], in0=binb[:, :], scalar1=0.5))

    def ln_group(g, src_ap, src_fam, with_router=True):
        tts = [g * GT + ti for ti in range(GT)]
        for tt in tts:
            ti = tt % GT
            dma("sync", dX[ti], xg[:, ti, :], src_ap[tt * 128:tt * 128 + 128, :],
                reads=([(src_fam, tt)] if src_fam else []), writes=[("xg", ti)])
        for tt in tts:
            ti = tt % GT
            ys = ti % 2
            kxg = ("xg", ti)
            kdV, kdS = ("ydmp", ys), ("ydmp", 2 + ys)
            p.step("vector", reads=[kxg], writes=[kdV, ("st", ti, 0)], sem=sV, inc=1,
                   f=lambda e, ti=ti, ys=ys: e.scalar_tensor_tensor(
                       out=ydmp[:, ys, :], in0=xg[:, ti, :], scalar=1.0 / H,
                       in1=xg[:, ti, :], op0=ALU.mult, op1=ALU.mult,
                       accum_out=stats[:, ti, 0:1]))
            p.step("scalar", reads=[kxg], writes=[kdS, ("st", ti, 1)], sem=sS, inc=1,
                   f=lambda e, ti=ti, ys=ys: e.activation(
                       out=ydmp[:, 2 + ys, :], in_=xg[:, ti, :], func=AF.Copy,
                       scale=1.0 / H, accum_out=stats[:, ti, 1:2]))
            p.step("vector", reads=[("st", ti, 1)], writes=[("st", ti, 2)], sem=sV, inc=1,
                   f=lambda e, ti=ti: e.tensor_scalar(
                       out=stats[:, ti, 2:3], in0=stats[:, ti, 1:2],
                       scalar1=stats[:, ti, 1:2], scalar2=-1.0,
                       op0=ALU.mult, op1=ALU.mult))
            p.step("vector", reads=[("st", ti, 0), ("st", ti, 2)], writes=[("st", ti, 3)],
                   sem=sV, inc=1,
                   f=lambda e, ti=ti: e.scalar_tensor_tensor(
                       out=stats[:, ti, 3:4], in0=stats[:, ti, 2:3], scalar=EPS_LN,
                       in1=stats[:, ti, 0:1], op0=ALU.add, op1=ALU.add))
            p.step("scalar", reads=[("st", ti, 3)], writes=[("st", ti, 4)], sem=sS, inc=1,
                   f=lambda e, ti=ti: e.activation(out=stats[:, ti, 4:5],
                                                   in_=stats[:, ti, 3:4], func=AF.Sqrt))
            p.step("vector", reads=[("st", ti, 4)], writes=[("st", ti, 5)], sem=sV, inc=1,
                   f=lambda e, ti=ti: e.reciprocal(out=stats[:, ti, 5:6],
                                                   in_=stats[:, ti, 4:5]))
            for hh in (0, 1):
                lo = hh * 1024
                pt = ps_c if hh == 0 else ps_d
                pn = "c" if hh == 0 else "d"
                pslots = [("psum", pn, 0), ("psum", pn, 1)]
                p.step("vector", reads=[kxg, ("st", ti, 1), ("st", ti, 5)],
                       writes=[("yf", ys, hh)], sem=sV, inc=1,
                       f=lambda e, ti=ti, ys=ys, lo=lo: e.tensor_scalar(
                           out=yf[:, ys, lo:lo + 1024], in0=xg[:, ti, lo:lo + 1024],
                           scalar1=stats[:, ti, 1:2], scalar2=stats[:, ti, 5:6],
                           op0=ALU.subtract, op1=ALU.mult))

                def tr_f(e, ys=ys, lo=lo, pt=pt):
                    last = None
                    for kk in range(8):
                        last = e.transpose(pt[:, kk * 128:kk * 128 + 128],
                                           yf[:, ys, lo + kk * 128:lo + kk * 128 + 128],
                                           idf[:, :])
                    return last
                p.step("tensor", reads=[("yf", ys, hh), ("idf", 0)], writes=pslots,
                       sem=sT, inc=1, f=tr_f)
                p.step("scalar", reads=pslots, writes=[("xlnT", ti)], sem=sS, inc=1,
                       f=lambda e, ti=ti, hh=hh, pt=pt: e.copy(
                           out=xlnT[:, hh * 8:hh * 8 + 8, ti * 128:ti * 128 + 128],
                           in_=pt[:, 0:1024].rearrange("p (k j) -> p k j", j=128)))
            if with_router:
                router_tile(tt)

    def router_tile(tt):
        ti = tt % GT
        pn = "a" if ti % 2 == 0 else "b"
        pst = PS[pn]

        def rt_f(e, ti=ti, pst=pst):
            last = None
            for k in range(KH):
                last = e.matmul(pst[:, 0:E], xlnT[:, k, ti * 128:ti * 128 + 128],
                                gwb[:, k, :], start=(k == 0), stop=(k == KH - 1))
            return last
        p.step("tensor", reads=[("xlnT", ti), ("gwb", 0)], writes=[("psum", pn, 0)],
               sem=sT, inc=1, f=rt_f)
        p.step("scalar", reads=[("psum", pn, 0)], writes=[("rtr", 0)], sem=sS, inc=1,
               f=lambda e, pst=pst: e.activation(out=rtr[:, 0, 0:E], in_=pst[:, 0:E],
                                                 func=AF.Sigmoid))
        p.step("vector", reads=[("rtr", 0), ("ebs", 0)], writes=[("rtr", 1)], sem=sV, inc=1,
               f=lambda e: e.tensor_add(out=rtr[:, 1, 0:E], in0=rtr[:, 0, 0:E], in1=ebs[:, :]))
        p.step("vector", reads=[("rtr", 1)], writes=[("rtr", 2)], sem=sV, inc=1,
               f=lambda e: e.tensor_copy(out=rtr[:, 2, 0:E], in_=rtr[:, 1, 0:E]))
        for j in range(TOPK - 1):
            p.step("vector", reads=[("rtr", 1)], writes=[("rst", 0)], sem=sV, inc=1,
                   f=lambda e: e.reduce_max(out=rst[:, 0:1], in_=rtr[:, 1, 0:E], axis=AX.X))
            p.step("vector", reads=[("rtr", 1), ("rst", 0)], writes=[("rtr", 3)], sem=sV, inc=1,
                   f=lambda e: e.tensor_scalar(out=rtr[:, 3, 0:E], in0=rtr[:, 1, 0:E],
                                               scalar1=rst[:, 0:1], scalar2=None,
                                               op0=ALU.is_equal))
            p.step("vector", reads=[("rtr", 1), ("rtr", 3)], writes=[("rtr", 1)], sem=sV, inc=1,
                   f=lambda e: e.scalar_tensor_tensor(out=rtr[:, 1, 0:E], in0=rtr[:, 3, 0:E],
                                                      scalar=-1e30, in1=rtr[:, 1, 0:E],
                                                      op0=ALU.mult, op1=ALU.add))
        p.step("vector", reads=[("rtr", 1)], writes=[("rst", 0)], sem=sV, inc=1,
               f=lambda e: e.reduce_max(out=rst[:, 0:1], in_=rtr[:, 1, 0:E], axis=AX.X))
        p.step("vector", reads=[("rtr", 2), ("rst", 0)], writes=[("rtr", 1)], sem=sV, inc=1,
               f=lambda e: e.tensor_scalar(out=rtr[:, 1, 0:E], in0=rtr[:, 2, 0:E],
                                           scalar1=rst[:, 0:1], scalar2=None, op0=ALU.is_ge))
        p.step("vector", reads=[("rtr", 0), ("rtr", 1)], writes=[("rtr", 0)], sem=sV, inc=1,
               f=lambda e: e.tensor_mul(out=rtr[:, 0, 0:E], in0=rtr[:, 0, 0:E],
                                        in1=rtr[:, 1, 0:E]))
        p.step("vector", reads=[("rtr", 0)], writes=[("rst", 1)], sem=sV, inc=1,
               f=lambda e: e.reduce_sum(out=rst[:, 1:2], in_=rtr[:, 0, 0:E], axis=AX.X))
        p.step("vector", reads=[("rst", 1)], writes=[("rst", 2)], sem=sV, inc=1,
               f=lambda e: e.reciprocal(out=rst[:, 2:3], in_=rst[:, 1:2]))
        p.step("vector", reads=[("rtr", 0), ("rst", 2)], writes=[("c", ti)], sem=sV, inc=1,
               f=lambda e, ti=ti: e.tensor_scalar(out=cbuf[:, ti, :], in0=rtr[:, 0, 0:E],
                                                  scalar1=rst[:, 2:3], scalar2=0.5,
                                                  op0=ALU.mult, op1=ALU.mult))

    blk = [0]
    stgi = [0]
    csel = [0]
    qsel = [0]

    def cast_step(eng, dst_ap, src_ap, reads, writes):
        sem = {"vector": sV, "scalar": sS}[eng]
        if eng == "scalar":
            p.step(eng, reads=reads, writes=writes, sem=sem, inc=1,
                   f=lambda e, d=dst_ap, s=src_ap: e.copy(out=d, in_=s))
        else:
            p.step(eng, reads=reads, writes=writes, sem=sem, inc=1,
                   f=lambda e, d=dst_ap, s=src_ap: e.tensor_copy(out=d, in_=s))

    def stage_load(src_f32, gur_slot, key_slot, cache_ap, cache_sem, ckey):
        slot = stgi[0] % 2
        stgi[0] += 1
        for h2 in (0, 1):
            lo = h2 * (KH // 2)
            kst = ("stg", slot, h2)
            dma(("sync", "scalar")[h2], dStg[slot * 2 + h2],
                stg[:, slot, lo * 128:(lo + KH // 2) * 128],
                src_f32[:, lo:lo + KH // 2, :].rearrange("p k j -> p (k j)"),
                writes=[kst])
            cast_step("scalar",
                      gur[:, gur_slot, lo:lo + KH // 2, :],
                      stg[:, slot, lo * 128:(lo + KH // 2) * 128]
                      .rearrange("p (k j) -> p k j", j=128),
                      reads=[kst], writes=[("gur", key_slot, h2)])
        dma("gpsimd", cache_sem, cache_ap, gur[:, gur_slot, :, :],
            reads=[("gur", key_slot, 0), ("gur", key_slot, 1)], writes=[ckey])

    def stream_pair(pair_src, pair_cache, ckbase, pass0):
        slot = blk[0] % NSLOT
        assert slot % 2 == 0
        blk[0] += 2
        ck0, ck1 = ckbase + (0,), ckbase + (1,)
        if pass0:
            for w in (0, 1):
                stage_load(pair_src[w], slot + w, slot + w,
                           pair_cache[w], dCw[slot + w], (ck0, ck1)[w])
        else:
            dma("sync", dGu[slot], gur[:, slot, :, :],
                pair_cache[0], reads=[ck0], writes=[("gur", slot, 0), ("gur", slot, 1)])
            dma("scalar", dGu[slot + 1], gur[:, slot + 1, :, :],
                pair_cache[1], reads=[ck1],
                writes=[("gur", slot + 1, 0), ("gur", slot + 1, 1)])
        return slot, slot + 1

    wsti = [0]

    def stream_wd_chunks(e_idx, k2):
        for k in (k2, k2 + 1):
            slot = wsti[0] % 2
            wsti[0] += 1
            kst = ("wdstg", slot)
            dma("gpsimd", dStgW[slot], wdstg[:, slot, :], wdd[e_idx, k], writes=[kst])
            cast_step("vector", wdsb[:, k, :], wdstg[:, slot, :],
                      reads=[kst], writes=[("wd", k)])
            dma("gpsimd", dCwd[k], cwd[e_idx, k], wdsb[:, k, :],
                reads=[("wd", k)], writes=[("cwd", e_idx, k)])

    XLK = [("xlnT", t2) for t2 in range(GT)]

    def mm_dmaj(slot, pn):
        ps_t = PS[pn]

        def f(e, slot=slot, ps_t=ps_t):
            last = None
            for k in range(KH):
                for (s0, w) in TCH:
                    last = e.matmul(ps_t[:, s0:s0 + w], gur[:, slot, k, :],
                                    xlnT[:, k, s0:s0 + w],
                                    start=(k == 0), stop=(k == KH - 1))
            return last
        p.step("tensor", reads=[("gur", slot, 0), ("gur", slot, 1)] + XLK,
               writes=[("psum", pn, s0 // 512) for (s0, _) in TCH],
               sem=sT, inc=1, f=f)

    def ffn_call(src_ap, src_fam, dst_ap, dst_fam, pass0_call):
        pref = {}
        for g in range(NG):
            pass0 = pass0_call and g == 0
            ln_group(g, src_ap, src_fam)
            for e_idx in range(E + 1):
                if not pass0:
                    for j, k2 in enumerate(range(0, KI, 2)):
                        dma("gpsimd", dWd[k2], wdsb[:, k2:k2 + 2, :],
                            cwd[e_idx, k2:k2 + 2].rearrange("w p n -> p w n"),
                            reads=[("cwd", e_idx, k2), ("cwd", e_idx, k2 + 1)],
                            writes=[("wd", k2), ("wd", k2 + 1)])
                def act_m(m):
                    pg, pu = ("a", "b") if m % 2 == 0 else ("c", "d")
                    ss = m % 2
                    for hh in (0, 1):
                        lo = hh * 512
                        p.step("scalar", reads=[("psum", pg, hh)],
                               writes=[("sgh", ss, hh)], sem=sS, inc=1,
                               f=lambda e, ss=ss, pg=pg, lo=lo: e.activation(
                                   out=sg[:, ss, lo:lo + 512],
                                   in_=PS[pg][:, lo:lo + 512], func=AF.Silu))
                        p.step("vector",
                               reads=[("sgh", ss, hh), ("psum", pu, hh)],
                               writes=[("hp", m, hh)], sem=sV, inc=1,
                               f=lambda e, ss=ss, m=m, pu=pu, lo=lo: e.tensor_mul(
                                   out=hp[:, m, lo:lo + 512],
                                   in0=sg[:, ss, lo:lo + 512],
                                   in1=PS[pu][:, lo:lo + 512]))
                for m in range(MI):
                    if (g, e_idx, m) in pref:
                        gslot, uslot = pref.pop((g, e_idx, m))
                    else:
                        gslot, uslot = stream_pair(wgu[e_idx, m], cgu[e_idx, m],
                                                   ("cgu", e_idx, m), pass0)
                    if m >= 1:
                        act_m(m - 1)
                    pg, pu = ("a", "b") if m % 2 == 0 else ("c", "d")
                    mm_dmaj(gslot, pg)
                    mm_dmaj(uslot, pu)
                    if pass0 and 1 <= m <= 4:
                        stream_wd_chunks(e_idx, (m - 1) * 2)
                act_m(MI - 1)
                nxt = (g, e_idx + 1) if e_idx < E else \
                      ((g + 1, 0) if g + 1 < NG else None)
                if nxt is not None:
                    ng, ne = nxt
                    np0 = pass0_call and ng == 0
                    for m2 in (0, 1):
                        pref[(ng, ne, m2)] = stream_pair(
                            wgu[ne, m2], cgu[ne, m2], ("cgu", ne, m2), np0)
                for ti in range(GT):
                    pr = ("a", "b") if ti % 2 == 0 else ("c", "d")
                    q0, q1 = PS[pr[0]], PS[pr[1]]

                    def ph2_f(e, ti=ti, q0=q0, q1=q1):
                        last = None
                        for k2 in range(KI):
                            for (s0, w) in HCH:
                                dst = q0 if s0 < 1024 else q1
                                ds0 = s0 % 1024
                                last = e.matmul(dst[:, ds0:ds0 + w],
                                                hp[:, k2, ti * 128:ti * 128 + 128],
                                                wdsb[:, k2, s0:s0 + w],
                                                start=(k2 == 0), stop=(k2 == KI - 1))
                        return last
                    dslots = sorted({("psum", pr[0] if s0 < 1024 else pr[1],
                                      (s0 % 1024) // 512) for (s0, _) in HCH})
                    p.step("tensor",
                           reads=[("hp", k2, ti // 4) for k2 in range(KI)]
                           + [("wd", k2) for k2 in range(KI)],
                           writes=dslots, sem=sT, inc=1, f=ph2_f)
                    for qi, qn in enumerate(pr):
                        lo = qi * 1024
                        if lo >= H:
                            break
                        w = min(1024, H - lo)
                        rd = [("psum", qn, 0)] + ([("psum", qn, 1)] if w > 512 else [])
                        if e_idx < E:
                            rd.append(("c", ti))
                        p.step("vector", reads=rd, writes=[("xg", ti)], sem=sV, inc=1,
                               f=lambda e, q=PS[qn], lo=lo, w=w, ti=ti, ee=e_idx:
                               e.scalar_tensor_tensor(
                                   out=xg[:, ti, lo:lo + w], in0=q[:, 0:w],
                                   scalar=(cbuf[:, ti, ee:ee + 1] if ee < E else 0.5),
                                   in1=xg[:, ti, lo:lo + w], op0=ALU.mult, op1=ALU.add))
            if pass0:
                for k2 in range(KI):
                    p.join_keys([("cwd", e2, k2) for e2 in range(E + 1)], dCwd[k2])
            for ti in range(GT):
                tt = g * GT + ti
                dma("scalar", dXW[ti], dst_ap[tt * 128:tt * 128 + 128, :], xg[:, ti, :],
                    reads=[("xg", ti)], writes=[(dst_fam, tt)])

    def griffin():
        p.step("vector", writes=[("zb", 0)], sem=sV, inc=1,
               f=lambda e: e.memset(zb[:, 0:TG], 0.0))

        def chain_m(m, g):
            r = m % 2
            pa, pb = ("a", "b") if m % 2 == 0 else ("c", "d")
            kpa = [("psum", pa, 0), ("psum", pa, 1)]
            kpb = [("psum", pb, 0), ("psum", pb, 1)]
            al, xb_ = ("gfa", r), ("gfa", 2 + r)
            hh, AA = ("gfa", 4 + r), ("gfa", 6 + r)
            p.step("scalar", reads=kpb + [("bgbh", 0)], writes=[("gsc", r)],
                   sem=sS, inc=1,
                   f=lambda e, r=r, m=m, pb=pb: e.activation(
                       out=gsc[:, r, :], in_=PS[pb][:, 0:TG],
                       func=AF.Tanh, scale=0.5, bias=bgbh[:, m:m + 1]))
            p.step("scalar", reads=kpa + [("binbh", 0)], writes=[("sg", r)],
                   sem=sS, inc=1,
                   f=lambda e, r=r, m=m, pa=pa: e.activation(
                       out=sg[:, r, :], in_=PS[pa][:, 0:TG],
                       func=AF.Tanh, scale=0.5, bias=binbh[:, m:m + 1]))
            p.step("vector", reads=[("gsc", r), ("sdmh", 0)], writes=[("gsc", 2 + r)],
                   sem=sV, inc=1,
                   f=lambda e, r=r, m=m: e.tensor_scalar(
                       out=gsc[:, 2 + r, :], in0=gsc[:, r, :],
                       scalar1=sdmh[:, m:m + 1], scalar2=sdmh[:, m:m + 1],
                       op0=ALU.mult, op1=ALU.add))
            p.step("scalar", reads=[("gsc", 2 + r)], writes=[al], sem=sS, inc=1,
                   f=lambda e, r=r: e.activation(out=gfa[:, r, :], in_=gsc[:, 2 + r, :],
                                                 func=AF.Exp, scale=-1.0))
            p.step("vector", reads=[al], writes=[("gsc", 2 + r)], sem=sV, inc=1,
                   f=lambda e, r=r: e.tensor_mul(out=gsc[:, 2 + r, :],
                                                 in0=gfa[:, r, :], in1=gfa[:, r, :]))
            p.step("scalar", reads=[("gsc", 2 + r), ("onec", 0)],
                   writes=[("gsc", 2 + r)], sem=sS, inc=1,
                   f=lambda e, r=r: e.activation(out=gsc[:, 2 + r, :],
                                                 in_=gsc[:, 2 + r, :], func=AF.Ln,
                                                 scale=-1.0, bias=onec[:, 0:1]))
            p.step("scalar", reads=[("gsc", 2 + r)], writes=[("gsc", 2 + r)],
                   sem=sS, inc=1,
                   f=lambda e, r=r: e.activation(out=gsc[:, 2 + r, :],
                                                 in_=gsc[:, 2 + r, :], func=AF.Exp,
                                                 scale=0.5))
            p.step("vector", reads=[("sg", r)], writes=[("sg", r)], sem=sV, inc=1,
                   f=lambda e, r=r: e.tensor_scalar(
                       out=sg[:, r, :], in0=sg[:, r, :], scalar1=0.5, scalar2=0.5,
                       op0=ALU.mult, op1=ALU.add))
            p.step("vector", reads=[("sg", r)] + XLK, writes=[xb_], sem=sV, inc=1,
                   f=lambda e, r=r, m=m: e.tensor_mul(out=gfa[:, 2 + r, :],
                                                      in0=sg[:, r, :],
                                                      in1=xlnT[:, m, :]))
            p.step("vector", reads=[xb_, ("gsc", 2 + r)], writes=[xb_], sem=sV, inc=1,
                   f=lambda e, r=r: e.tensor_mul(out=gfa[:, 2 + r, :],
                                                 in0=gfa[:, 2 + r, :],
                                                 in1=gsc[:, 2 + r, :]))
            p.step("vector", reads=[al, xb_] + ([("hend", m)] if g else []),
                   writes=[hh], sem=sV, inc=1,
                   f=lambda e, r=r, m=m, g=g: e.tensor_tensor_scan(
                       out=gfa[:, 4 + r, :], data0=gfa[:, r, :],
                       data1=gfa[:, 2 + r, :],
                       initial=(0.0 if g == 0 else hend[:, m:m + 1]),
                       op0=ALU.mult, op1=ALU.add))
            p.step("vector", reads=[al, ("zb", 0)] + ([("aend", m)] if g else []),
                   writes=[AA], sem=sV, inc=1,
                   f=lambda e, r=r, m=m, g=g: e.tensor_tensor_scan(
                       out=gfa[:, 6 + r, :], data0=gfa[:, r, :], data1=zb[:, 0:TG],
                       initial=(1.0 if g == 0 else aend[:, m:m + 1]),
                       op0=ALU.mult, op1=ALU.add))
            p.step("vector", reads=[hh], writes=[("hend", m)], sem=sV, inc=1,
                   f=lambda e, r=r, m=m: e.tensor_copy(out=hend[:, m:m + 1],
                                                       in_=gfa[:, 4 + r, TG - 1:TG]))
            p.step("vector", reads=[AA], writes=[("aend", m)], sem=sV, inc=1,
                   f=lambda e, r=r, m=m: e.tensor_copy(out=aend[:, m:m + 1],
                                                       in_=gfa[:, 6 + r, TG - 1:TG]))
            dma("gpsimd", dHdH[r], hdram[m, :, g * TG:(g + 1) * TG], gfa[:, 4 + r, :],
                reads=[hh], writes=[("hd", m, g)])
            dma("scalar", dHdA[r], adram[m, :, g * TG:(g + 1) * TG], gfa[:, 6 + r, :],
                reads=[AA], writes=[("ad", m, g)])

        for g in range(NG):
            pass0 = g == 0
            ln_group(g, xr, "xr", with_router=False)
            prev = None
            for m in range(KH):
                islot, gslot = stream_pair(wio[m], cio[m], ("cio", m), pass0)
                if prev is not None:
                    chain_m(prev, g)
                pa, pb = ("a", "b") if m % 2 == 0 else ("c", "d")
                mm_dmaj(islot, pa)
                mm_dmaj(gslot, pb)
                prev = m
            chain_m(KH - 1, g)
        for r in range(2):
            p.join_keys([("hd", m, g) for m in range(KH) for g in range(NG) if m % 2 == r],
                        dHdH[r])
            p.join_keys([("ad", m, g) for m in range(KH) for g in range(NG) if m % 2 == r],
                        dHdA[r])

        def xchg_f(e):
            with e.register("hreg") as hreg, e.register("cur") as cur, \
                 e.register("tgt") as tgt, e.register("cnt") as cnt, \
                 e.register("go") as go:
                e.reg_load(hreg, hsb[0:1, 0:1])
                myslot = e.snap(hreg, min_val=0, max_val=1)
                e.dma_start(out=shd[bass.ds(myslot, 1), :, :],
                            in_=hend[:, :]).then_inc(dPoll, 16)
                e.wait_ge(dPoll, 16)
                e.dma_start(out=shf[bass.ds(myslot, 1), :, :],
                            in_=nsb[:, 0:1]).then_inc(dPoll, 16)
                e.wait_ge(dPoll, 32)
                e.reg_load(tgt, nsb[0:1, 0:1])
                e.reg_mov(cnt, 32)
                e.reg_mov(go, 1)
                with e.While(go):
                    e.dma_start(out=fsb[:, 0:1],
                                in_=shf[0:1, :, :]).then_inc(dPoll, 16)
                    e.reg_add(cnt, cnt, 16)
                    e.wait_ge(dPoll, cnt)
                    e.reg_load(cur, fsb[0:1, 0:1])
                    e.reg_alu(go, cur, tgt, ALU.not_equal)
                return e.dma_start(out=ccrs[:, :], in_=shd[0:1, :, :])
        for m in (0, 1):
            r = m % 2
            dma("sync", dHb[r], gfb[:, r, :], hdram[m, :, TG:2 * TG],
                reads=[("hd", m, 1)], writes=[("gfb", r)])
            dma("scalar", dAb[r], gfb[:, 2 + r, :], adram[m, :, TG:2 * TG],
                reads=[("ad", m, 1)], writes=[("gfb", 2 + r)])
        p.step("gpsimd", reads=[("hend", m) for m in range(KH)]
               + [("nsb", 0), ("hsb", 0)],
               writes=[("ccrs", 0)], sem=dCc, inc=16, f=xchg_f)
        p.step("vector", reads=[("ccrs", 0), ("flg", 0)], writes=[("ceff", 0)], sem=sV, inc=1,
               f=lambda e: e.tensor_scalar_mul(out=ceff[:, :], in0=ccrs[:, :],
                                               scalar1=flg[:, 0:1]))

        for g in (1, 0):
            if g == 0:
                for ti in range(GT):
                    tt = g * GT + ti
                    dma("gpsimd", dX[ti], xg[:, ti, :], xr[tt * 128:tt * 128 + 128, :],
                        reads=[("xr", tt)], writes=[("xg", ti)])
            for m in range(KH):
                r = m % 2
                kh_, ka = ("gfb", r), ("gfb", 2 + r)
                if not (g == 1 and m < 2):
                    dma("sync", dHb[r], gfb[:, r, :], hdram[m, :, g * TG:(g + 1) * TG],
                        reads=[("hd", m, g)], writes=[kh_])
                    dma("scalar", dAb[r], gfb[:, 2 + r, :],
                        adram[m, :, g * TG:(g + 1) * TG],
                        reads=[("ad", m, g)], writes=[ka])
                kht = ("ght", r)
                p.step("vector", reads=[kh_, ka, ("ceff", 0)], writes=[kht], sem=sV, inc=1,
                       f=lambda e, r=r, m=m: e.scalar_tensor_tensor(
                           out=ght[:, r, :], in0=gfb[:, 2 + r, :], scalar=ceff[:, m:m + 1],
                           in1=gfb[:, r, :], op0=ALU.mult, op1=ALU.add))
                pn = "c" if r == 0 else "d"
                pst = PS[pn]
                psk = [("psum", pn, 0), ("psum", pn, 1)]

                def htr_f(e, r=r, pst=pst):
                    last = None
                    for t2 in range(GT):
                        last = e.transpose(pst[:, t2 * 128:t2 * 128 + 128],
                                           ght[:, r, t2 * 128:t2 * 128 + 128], idf[:, :])
                    return last
                p.step("tensor", reads=[kht, ("idf", 0)], writes=psk, sem=sT, inc=1, f=htr_f)
                xgk = [("xg", t2) for t2 in range(GT)]
                p.step("vector", reads=psk + xgk, writes=xgk, sem=sV, inc=1,
                       f=lambda e, m=m, pst=pst: e.tensor_add(
                           out=xg[:, :, m * 128:m * 128 + 128],
                           in0=xg[:, :, m * 128:m * 128 + 128],
                           in1=pst[:, 0:GT * 128].rearrange("p (t j) -> p t j", j=128)))
            for ti in range(GT):
                tt = g * GT + ti
                dma("scalar", dXW[ti], xr[tt * 128:tt * 128 + 128, :], xg[:, ti, :],
                    reads=[("xg", ti)], writes=[("xr", tt)])

    ffn_call(x_in, None, xr, "xr", pass0_call=True)
    griffin()
    ffn_call(xr, "xr", y_out, "yd", pass0_call=False)

    for t in range(GT):
        p.wait_abs("gpsimd", dXW[t], dXW[t].n)

    with st:
        block = nc.Block()
        with block:
            p.emit(block)
    return nc



_CACHE = {}
_NONCE = [int(np.random.SeedSequence().entropy % 2000000000) + 1]


def kernel(L0_attn_x, L0_attn_W_in, L0_attn_b_in, L0_attn_W_gate, L0_attn_b_gate,
           L0_attn_forget_lambda, L0_ffn_gate_w, L0_ffn_expert_bias, L0_ffn_w_gate,
           L0_ffn_w_up, L0_ffn_w_down, L0_ffn_sw_gate, L0_ffn_sw_up, L0_ffn_sw_down):
    cfg = derived(full_cfg())
    H, E, KH, TLOC = cfg["H"], cfg["E"], cfg["KH"], cfg["TLOC"]
    B = 4

    x = np.ascontiguousarray(np.asarray(L0_attn_x, dtype=np.float32))
    arr_gu, arr_wd, arr_io, arr_gw = pack_weights(
        cfg,
        np.asarray(L0_ffn_w_gate, np.float32), np.asarray(L0_ffn_w_up, np.float32),
        np.asarray(L0_ffn_w_down, np.float32), np.asarray(L0_ffn_sw_gate, np.float32),
        np.asarray(L0_ffn_sw_up, np.float32), np.asarray(L0_ffn_sw_down, np.float32),
        np.asarray(L0_attn_W_in, np.float32), np.asarray(L0_attn_W_gate, np.float32),
        np.asarray(L0_ffn_gate_w, np.float32))
    eb = np.ascontiguousarray(np.broadcast_to(
        np.asarray(L0_ffn_expert_bias, np.float32), (128, E)))
    bin_a = np.ascontiguousarray(np.asarray(L0_attn_b_in, np.float32).reshape(KH, 128).T)
    bg_a = np.ascontiguousarray(np.asarray(L0_attn_b_gate, np.float32).reshape(KH, 128).T)
    lam_a = np.ascontiguousarray(
        np.asarray(L0_attn_forget_lambda, np.float32).reshape(KH, 128).T)
    identf = np.eye(128, dtype=np.float32)

    if "nc" not in _CACHE:
        _CACHE["nc"] = build_program(full_cfg())
    nc = _CACHE["nc"]
    _NONCE[0] = (_NONCE[0] * 1103515245 + 12345) % 2147483629 + 1

    xs = x.reshape(B, 2, TLOC, H)
    in_maps = []
    for c in range(8):
        b, half = c // 2, c % 2
        in_maps.append(dict(
            x=np.ascontiguousarray(xs[b, half]), wgu=arr_gu, wd=arr_wd, wio=arr_io,
            gw=arr_gw, ebias=eb, bin=bin_a, bgate=bg_a, lam=lam_a,
            flag=np.full((128, 1), float(half), np.float32),
            nflag=np.full((128, 1), 1.0 - float(half), np.float32),
            identf=identf,
            onec=np.full((128, 1), 1.0 + 1e-6, np.float32),
            nonce=np.array([[_NONCE[0]]], np.int32),
            halfi=np.array([[half]], np.int32)))
    res = run_bass_kernel_spmd(nc, in_maps, core_ids=list(range(8)))
    out = np.empty((B, 4096, H), np.float32)
    for c in range(8):
        b, half = c // 2, c % 2
        out[b, half * TLOC:(half + 1) * TLOC, :] = res.results[c]["y"]
    return out


# version 24
# speedup vs baseline: 1.0627x; 1.0627x over previous
import numpy as np
import ml_dtypes
import concourse.bass as bass
import concourse.mybir as mybir
from concourse.bass_utils import run_bass_kernel_spmd
from contextlib import ExitStack

F32 = mybir.dt.float32
I32 = mybir.dt.int32
BF16 = mybir.dt.bfloat16
AF = mybir.ActivationFunctionType
ALU = mybir.AluOpType
AX = mybir.AxisListType


def full_cfg():
    return dict(H=2048, I=1024, E=16, TOPK=4, GT=8, NG=2,
                PAIRS=[[0, 1], [2, 3], [4, 5], [6, 7]])


def derived(cfg):
    d = dict(cfg)
    d["KH"] = cfg["H"] // 128
    d["KI"] = cfg["I"] // 128
    d["TG"] = cfg["GT"] * 128
    d["TILES"] = cfg["GT"] * cfg["NG"]
    d["TLOC"] = d["TILES"] * 128
    d["NSLOT"] = 4
    return d



def pack_weights(cfg, wg, wu, wd, sw_g, sw_u, sw_d, W_in, W_gate, gate_w):
    H, I, E, KH, KI = cfg["H"], cfg["I"], cfg["E"], cfg["KH"], cfg["KI"]
    MI = KI
    wg_all = np.concatenate([wg, sw_g[None]], 0)
    wu_all = np.concatenate([wu, sw_u[None]], 0)
    wd_all = np.concatenate([wd, sw_d[None]], 0)

    def gu(a):
        return a.reshape(E + 1, MI, 128, KH, 128).transpose(0, 1, 4, 3, 2)

    arr_gu = np.ascontiguousarray(
        np.stack([gu(wg_all), gu(wu_all)], axis=2))
    arr_wd = np.ascontiguousarray(
        wd_all.reshape(E + 1, H, KI, 128).transpose(0, 2, 3, 1))

    def io(W):
        return W.reshape(KH, 128, KH, 128).transpose(0, 3, 2, 1)

    arr_io = np.ascontiguousarray(
        np.stack([io(W_in), io(W_gate)], axis=1))
    arr_gw = np.ascontiguousarray(gate_w.T.reshape(KH, 128, E))
    return arr_gu, arr_wd, arr_io, arr_gw



ENGS = ("tensor", "vector", "scalar", "gpsimd", "sync")


class Sem:
    __slots__ = ("h", "n", "name")

    def __init__(self, h, name):
        self.h = h
        self.n = 0
        self.name = name


class Plan:

    def __init__(self):
        self.ops = {e: [] for e in ENGS}
        self.last_w = {}
        self.readers = {}
        self.hw = {e: {} for e in ENGS}
        self.region_of = {}
        self.region_keys = {}

    def _cands(self, key):
        fam = key[0]
        out = [key]
        reg = self.region_of.get(fam)
        if reg is not None:
            for f2, ks in self.region_keys.get(reg, {}).items():
                if f2 != fam:
                    out.extend(ks)
        return out

    def _register(self, key):
        fam = key[0]
        reg = self.region_of.get(fam)
        if reg is not None:
            self.region_keys.setdefault(reg, {}).setdefault(fam, set()).add(key)

    def step(self, eng, reads=(), writes=(), sem=None, inc=0, f=None):
        need = {}

        def add(marks):
            for (s, v) in marks:
                if v > need.get(s, 0):
                    need[s] = v
        for k in reads:
            for c in self._cands(k):
                if c in self.last_w:
                    add([self.last_w[c]])
        for k in writes:
            for c in self._cands(k):
                if c in self.last_w:
                    add([self.last_w[c]])
                if c in self.readers:
                    add(self.readers[c])
        hw = self.hw[eng]
        waits = []
        for s, v in sorted(need.items(), key=lambda kv: kv[0].name):
            if hw.get(s.name, 0) < v:
                waits.append((s, v))
                hw[s.name] = v
        if sem is not None:
            sem.n += inc
            marker = (sem, sem.n)
        else:
            marker = None
        self.ops[eng].append((waits, f, sem, inc))
        if marker is not None:
            for k in writes:
                self._register(k)
                self.last_w[k] = marker
                self.readers[k] = []
            for k in reads:
                self._register(k)
                self.readers.setdefault(k, []).append(marker)
        else:
            assert not writes
        return marker

    def join_keys(self, keys, sem):
        for k in keys:
            if k in self.last_w and self.last_w[k][0] is sem:
                self.last_w[k] = (sem, sem.n)

    def wait_abs(self, eng, sem, val):
        hw = self.hw[eng]
        if hw.get(sem.name, 0) < val:
            hw[sem.name] = val
            self.ops[eng].append(([(sem, val)], None, None, 0))

    def emit(self, block):
        def runner(lst):
            def run(engine):
                for waits, f, sem, inc in lst:
                    for (s, v) in waits:
                        engine.wait_ge(s.h, v)
                    if f is not None:
                        instr = f(engine)
                        if sem is not None:
                            instr.then_inc(sem.h, inc)
            return run
        block.sync(runner(self.ops["sync"]))
        block.tensor(runner(self.ops["tensor"]))
        block.vector(runner(self.ops["vector"]))
        block.scalar(runner(self.ops["scalar"]))
        block.gpsimd(runner(self.ops["gpsimd"]))



def build_program(cfg, detect_races=False):
    cfg = derived(cfg)
    H, I, E, KH, KI = cfg["H"], cfg["I"], cfg["E"], cfg["KH"], cfg["KI"]
    MI, GT, NG, TG = KI, cfg["GT"], cfg["NG"], cfg["TG"]
    TILES, TLOC, NSLOT = cfg["TILES"], cfg["TLOC"], cfg["NSLOT"]
    TOPK = cfg["TOPK"]
    TCH = [(s, min(512, TG - s)) for s in range(0, TG, 512)]
    HCH = [(s, min(512, H - s)) for s in range(0, H, 512)]
    EPS_LN = 1e-5

    nc = bass.Bass("TRN2", target_bir_lowering=False, debug=False,
                   detect_race_conditions=detect_races)
    arena = nc.alloc_sbuf_tensor("arena0", [128, 206 * 1024], mybir.dt.uint8)
    ABASE = nc.lookup_mloc(arena).addr

    x_in = nc.dram_tensor("x", [TLOC, H], F32, kind="ExternalInput").ap()
    wgu = nc.dram_tensor("wgu", [E + 1, MI, 2, 128, KH, 128], F32, kind="ExternalInput").ap()
    wdd = nc.dram_tensor("wd", [E + 1, KI, 128, H], F32, kind="ExternalInput").ap()
    wio = nc.dram_tensor("wio", [KH, 2, 128, KH, 128], F32, kind="ExternalInput").ap()
    gw = nc.dram_tensor("gw", [KH, 128, E], F32, kind="ExternalInput").ap()
    ebias = nc.dram_tensor("ebias", [128, E], F32, kind="ExternalInput").ap()
    bin_d = nc.dram_tensor("bin", [128, KH], F32, kind="ExternalInput").ap()
    bgate_d = nc.dram_tensor("bgate", [128, KH], F32, kind="ExternalInput").ap()
    lam_d = nc.dram_tensor("lam", [128, KH], F32, kind="ExternalInput").ap()
    flag_d = nc.dram_tensor("flag", [128, 1], F32, kind="ExternalInput").ap()
    nflag_d = nc.dram_tensor("nflag", [128, 1], F32, kind="ExternalInput").ap()
    ident_d = nc.dram_tensor("identf", [128, 128], F32, kind="ExternalInput").ap()
    y_out = nc.dram_tensor("y", [TLOC, H], F32, kind="ExternalOutput").ap()

    xr = nc.dram_tensor("xr", [TLOC, H], F32).ap()
    cgu = nc.dram_tensor("cgu", [E + 1, MI, 2, 128, KH, 128], BF16).ap()
    cwd = nc.dram_tensor("cwd", [E + 1, KI, 128, H], BF16).ap()
    cio = nc.dram_tensor("cio", [KH, 2, 128, KH, 128], BF16).ap()
    hdram = nc.dram_tensor("hdram", [KH, 128, TLOC], F32).ap()
    adram = nc.dram_tensor("adram", [KH, 128, TLOC], F32).ap()
    shd = nc.dram_tensor("shd", [2, 128, KH], F32, addr_space="Shared").ap()
    shf = nc.dram_tensor("shf", [2, 1, 1], I32, addr_space="Shared").ap()
    onec_d = nc.dram_tensor("onec", [128, 1], F32, kind="ExternalInput").ap()
    nonce_d = nc.dram_tensor("nonce", [1, 1], I32, kind="ExternalInput").ap()
    half_d = nc.dram_tensor("halfi", [1, 1], I32, kind="ExternalInput").ap()

    st = ExitStack()
    EE = st.enter_context

    off = [0]

    def alloc(name, shape, dtype, at=None):
        per = int(np.prod(shape[1:])) * mybir.dt.size(dtype)
        if at is None:
            at = off[0]
            off[0] = at + ((per + 127) // 128) * 128
        t = nc.alloc_sbuf_tensor_at(name, list(shape), dtype, offset=ABASE + at)
        return t, at

    def region(n_bytes):
        at = off[0]
        off[0] = at + ((n_bytes + 127) // 128) * 128
        return at

    xg, _ = alloc("xg", [128, GT, H], F32)
    xlnT, _ = alloc("xlnT", [128, KH, TG], BF16)
    r_wd = region(max(KI * H * 2, 8 * TG * 4, 4 * H * 4))
    wdsb, _ = alloc("wdsb", [128, KI, H], BF16, at=r_wd)
    gfa, _ = alloc("gfa", [128, 8, TG], F32, at=r_wd)
    ydmp, _ = alloc("ydmp", [128, 4, H], F32, at=r_wd)
    r_hp = region(max(MI * TG * 2, 4 * TG * 4))
    hp, _ = alloc("hp", [128, MI, TG], BF16, at=r_hp)
    gfb, _ = alloc("gfb", [128, 4, TG], F32, at=r_hp)
    zb, _ = alloc("zb", [128, TG], F32, at=r_hp)
    gur, _ = alloc("gur", [128, NSLOT, KH, 128], BF16)
    stg, _ = alloc("stg", [128, 2, H], F32)
    r_yf = region(max(2 * H * 4, 4 * TG * 4))
    yf, _ = alloc("yf", [128, 2, H], F32, at=r_yf)
    wdstg, _ = alloc("wdstg", [128, 2, H], F32, at=r_yf)
    ght, _ = alloc("ght", [128, 2, TG], F32, at=r_yf)
    gsc, _ = alloc("gsc", [128, 4, TG], F32, at=r_yf)
    sg, _ = alloc("sg", [128, 2, TG], F32)
    cbuf, _ = alloc("cbuf", [128, GT, E], F32)
    stats, _ = alloc("stats", [128, GT, 8], F32)
    rtr, _ = alloc("rtr", [128, 4, max(E, 4)], F32)
    rst, _ = alloc("rst", [128, 4], F32)
    gwb, _ = alloc("gwb", [128, KH, E], BF16)
    gwf, _ = alloc("gwf", [128, KH, E], F32)
    ebs, _ = alloc("ebs", [128, E], F32)
    binb, _ = alloc("binb", [128, KH], F32)
    bgb, _ = alloc("bgb", [128, KH], F32)
    lamb, _ = alloc("lamb", [128, KH], F32)
    sdm, _ = alloc("sdm", [128, KH], F32)
    sdmh, _ = alloc("sdmh", [128, KH], F32)
    bgbh, _ = alloc("bgbh", [128, KH], F32)
    binbh, _ = alloc("binbh", [128, KH], F32)
    tA, _ = alloc("tA", [128, KH], F32)
    tB, _ = alloc("tB", [128, KH], F32)
    flg, _ = alloc("flg", [128, 1], F32)
    onec, _ = alloc("onec", [128, 1], F32)
    nflg, _ = alloc("nflg", [128, 1], F32)
    idf, _ = alloc("idf", [128, 128], F32)
    hend, _ = alloc("hend", [128, KH], F32)
    aend, _ = alloc("aend", [128, KH], F32)
    ceff, _ = alloc("ceff", [128, KH], F32)
    ccrs, _ = alloc("ccrs", [128, KH], F32)
    nsb, _ = alloc("nsb", [1, 4], I32)
    hsb, _ = alloc("hsb", [1, 4], I32)
    fsb, _ = alloc("fsb", [1, 4], I32)
    assert off[0] <= 206 * 1024, f"sbuf over budget: {off[0]/1024:.1f} KiB"

    ps_a = EE(nc.psum_tensor("ps_a", [128, 1024], F32))
    ps_b = EE(nc.psum_tensor("ps_b", [128, 1024], F32))
    ps_c = EE(nc.psum_tensor("ps_c", [128, 1024], F32))
    ps_d = EE(nc.psum_tensor("ps_d", [128, 1024], F32))
    PS = {"a": ps_a, "b": ps_b, "c": ps_c, "d": ps_d}

    p = Plan()
    p.region_of.update({
        "wd": "R_wd", "gfa": "R_wd", "ydmp": "R_wd",
        "hp": "R_hp", "gfb": "R_hp", "zb": "R_hp",
        "yf": "R_yf", "ght": "R_yf", "gsc": "R_yf", "wdstg": "R_yf",
        "sg": "R_sg", "sgh": "R_sg",
    })

    def mksem(name):
        return Sem(EE(nc.semaphore(name)), name)

    sT, sV, sS, sG = mksem("sT"), mksem("sV"), mksem("sS"), mksem("sG")
    dCst = mksem("dCst")
    dX = [mksem(f"dX{t}") for t in range(GT)]
    dXW = [mksem(f"dXW{t}") for t in range(GT)]
    dGu = [mksem(f"dGu{s}") for s in range(NSLOT)]
    dCw = [mksem(f"dCw{s}") for s in range(NSLOT)]
    dStg = [mksem(f"dStg{s}") for s in range(4)]
    dStgW = [mksem(f"dStgW{s}") for s in range(2)]
    dWd = [mksem(f"dWd{k}") for k in range(KI)]
    dCwd = [mksem(f"dCwd{k}") for k in range(KI)]
    dHdH = [mksem(f"dHdH{r}") for r in range(2)]
    dHdA = [mksem(f"dHdA{r}") for r in range(2)]
    dHb = [mksem(f"dHb{s}") for s in range(2)]
    dAb = [mksem(f"dAb{s}") for s in range(2)]
    dCc = mksem("dCc")
    dPoll = EE(nc.semaphore("dPoll"))

    def dma(eng, sem, out_ap, in_ap, reads=(), writes=()):
        return p.step(eng, reads=reads, writes=writes, sem=sem, inc=16,
                      f=lambda e, o=out_ap, i=in_ap: e.dma_start(out=o, in_=i))

    const_keys = [("gwf", 0), ("ebs", 0), ("binb", 0), ("bgb", 0), ("lamb", 0),
                  ("flg", 0), ("nflg", 0), ("idf", 0), ("onec", 0)]
    dma("sync", dCst, gwf[:, :, :], gw.rearrange("k p e -> p k e"), writes=[("gwf", 0)])
    dma("sync", dCst, ebs[:, :], ebias, writes=[("ebs", 0)])
    dma("sync", dCst, binb[:, :], bin_d, writes=[("binb", 0)])
    dma("sync", dCst, bgb[:, :], bgate_d, writes=[("bgb", 0)])
    dma("sync", dCst, lamb[:, :], lam_d, writes=[("lamb", 0)])
    dma("sync", dCst, flg[:, :], flag_d, writes=[("flg", 0)])
    dma("sync", dCst, onec[:, :], onec_d, writes=[("onec", 0)])
    dma("sync", dCst, nflg[:, :], nflag_d, writes=[("nflg", 0)])
    dma("sync", dCst, idf[:, :], ident_d, writes=[("idf", 0)])
    dma("sync", dCst, nsb[:, 0:1], nonce_d[0], writes=[("nsb", 0)])
    dma("sync", dCst, hsb[:, 0:1], half_d[0], writes=[("hsb", 0)])
    const_keys.extend([("nsb", 0), ("hsb", 0)])
    p.join_keys(const_keys, dCst)

    p.step("vector", reads=[("gwf", 0)], writes=[("gwb", 0)], sem=sV, inc=1,
           f=lambda e: e.tensor_copy(out=gwb[:, :, :], in_=gwf[:, :, :]))
    p.step("scalar", reads=[("lamb", 0)], writes=[("tA", 0)], sem=sS, inc=1,
           f=lambda e: e.activation(out=tA[:, :], in_=lamb[:, :], func=AF.Abs))
    p.step("scalar", reads=[("tA", 0)], writes=[("tA", 0)], sem=sS, inc=1,
           f=lambda e: e.activation(out=tA[:, :], in_=tA[:, :], func=AF.Exp, scale=-1.0))
    p.step("vector", reads=[("tA", 0)], writes=[("tA", 0)], sem=sV, inc=1,
           f=lambda e: e.tensor_scalar_add(out=tA[:, :], in0=tA[:, :], scalar1=1.0))
    p.step("scalar", reads=[("tA", 0)], writes=[("tA", 0)], sem=sS, inc=1,
           f=lambda e: e.activation(out=tA[:, :], in_=tA[:, :], func=AF.Ln))
    p.step("vector", reads=[("lamb", 0)], writes=[("tB", 0)], sem=sV, inc=1,
           f=lambda e: e.tensor_scalar_max(out=tB[:, :], in0=lamb[:, :], scalar1=0.0))
    p.step("vector", reads=[("tA", 0), ("tB", 0)], writes=[("tB", 0)], sem=sV, inc=1,
           f=lambda e: e.tensor_add(out=tB[:, :], in0=tB[:, :], in1=tA[:, :]))
    p.step("vector", reads=[("tB", 0)], writes=[("sdm", 0)], sem=sV, inc=1,
           f=lambda e: e.tensor_scalar_mul(out=sdm[:, :], in0=tB[:, :], scalar1=8.0))
    p.step("vector", reads=[("tB", 0)], writes=[("sdmh", 0)], sem=sV, inc=1,
           f=lambda e: e.tensor_scalar_mul(out=sdmh[:, :], in0=tB[:, :], scalar1=4.0))
    p.step("vector", reads=[("bgb", 0)], writes=[("bgbh", 0)], sem=sV, inc=1,
           f=lambda e: e.tensor_scalar_mul(out=bgbh[:, :], in0=bgb[:, :], scalar1=0.5))
    p.step("vector", reads=[("binb", 0)], writes=[("binbh", 0)], sem=sV, inc=1,
           f=lambda e: e.tensor_scalar_mul(out=binbh[:, :], in0=binb[:, :], scalar1=0.5))

    def ln_group(g, src_ap, src_fam, with_router=True, resident=False):
        tts = [g * GT + ti for ti in range(GT)]
        if not resident:
            for tt in tts:
                ti = tt % GT
                dma("sync", dX[ti], xg[:, ti, :], src_ap[tt * 128:tt * 128 + 128, :],
                    reads=([(src_fam, tt)] if src_fam else []), writes=[("xg", ti)])
        for tt in tts:
            ti = tt % GT
            ys = ti % 2
            kxg = ("xg", ti)
            kdV = ("ydmp", ys)
            p.step("vector", reads=[kxg], writes=[kdV, ("st", ti, 0)], sem=sV, inc=1,
                   f=lambda e, ti=ti, ys=ys: e.scalar_tensor_tensor(
                       out=ydmp[:, ys, :], in0=xg[:, ti, :], scalar=1.0 / H,
                       in1=xg[:, ti, :], op0=ALU.mult, op1=ALU.mult,
                       accum_out=stats[:, ti, 0:1]))
        for tt in tts:
            ti = tt % GT
            ys = ti % 2
            kxg = ("xg", ti)
            kdS = ("ydmp", 2 + ys)
            p.step("scalar", reads=[kxg], writes=[kdS, ("st", ti, 1)], sem=sS, inc=1,
                   f=lambda e, ti=ti, ys=ys: e.activation(
                       out=ydmp[:, 2 + ys, :], in_=xg[:, ti, :], func=AF.Copy,
                       scale=1.0 / H, accum_out=stats[:, ti, 1:2]))
        for tt in tts:
            ti = tt % GT
            p.step("vector", reads=[("st", ti, 1)], writes=[("st", ti, 2)], sem=sV, inc=1,
                   f=lambda e, ti=ti: e.tensor_scalar(
                       out=stats[:, ti, 2:3], in0=stats[:, ti, 1:2],
                       scalar1=stats[:, ti, 1:2], scalar2=-1.0,
                       op0=ALU.mult, op1=ALU.mult))
            p.step("vector", reads=[("st", ti, 0), ("st", ti, 2)], writes=[("st", ti, 3)],
                   sem=sV, inc=1,
                   f=lambda e, ti=ti: e.scalar_tensor_tensor(
                       out=stats[:, ti, 3:4], in0=stats[:, ti, 2:3], scalar=EPS_LN,
                       in1=stats[:, ti, 0:1], op0=ALU.add, op1=ALU.add))
        for tt in tts:
            ti = tt % GT
            p.step("scalar", reads=[("st", ti, 3)], writes=[("st", ti, 4)], sem=sS, inc=1,
                   f=lambda e, ti=ti: e.activation(out=stats[:, ti, 4:5],
                                                   in_=stats[:, ti, 3:4], func=AF.Sqrt))
        for tt in tts:
            ti = tt % GT
            p.step("vector", reads=[("st", ti, 4)], writes=[("st", ti, 5)], sem=sV, inc=1,
                   f=lambda e, ti=ti: e.reciprocal(out=stats[:, ti, 5:6],
                                                   in_=stats[:, ti, 4:5]))
        for tt in tts:
            ti = tt % GT
            ys = ti % 2
            kxg = ("xg", ti)
            for hh in (0, 1):
                lo = hh * 1024
                pt = ps_c if hh == 0 else ps_d
                pn = "c" if hh == 0 else "d"
                pslots = [("psum", pn, 0), ("psum", pn, 1)]
                p.step("vector", reads=[kxg, ("st", ti, 1), ("st", ti, 5)],
                       writes=[("yf", ys, hh)], sem=sV, inc=1,
                       f=lambda e, ti=ti, ys=ys, lo=lo: e.tensor_scalar(
                           out=yf[:, ys, lo:lo + 1024], in0=xg[:, ti, lo:lo + 1024],
                           scalar1=stats[:, ti, 1:2], scalar2=stats[:, ti, 5:6],
                           op0=ALU.subtract, op1=ALU.mult))

                def tr_f(e, ys=ys, lo=lo, pt=pt):
                    last = None
                    for kk in range(8):
                        last = e.transpose(pt[:, kk * 128:kk * 128 + 128],
                                           yf[:, ys, lo + kk * 128:lo + kk * 128 + 128],
                                           idf[:, :])
                    return last
                p.step("tensor", reads=[("yf", ys, hh), ("idf", 0)], writes=pslots,
                       sem=sT, inc=1, f=tr_f)
                p.step("scalar", reads=pslots, writes=[("xlnT", ti)], sem=sS, inc=1,
                       f=lambda e, ti=ti, hh=hh, pt=pt: e.copy(
                           out=xlnT[:, hh * 8:hh * 8 + 8, ti * 128:ti * 128 + 128],
                           in_=pt[:, 0:1024].rearrange("p (k j) -> p k j", j=128)))
            if with_router:
                router_tile(tt)

    def router_tile(tt):
        ti = tt % GT
        pn = "a" if ti % 2 == 0 else "b"
        pst = PS[pn]

        def rt_f(e, ti=ti, pst=pst):
            last = None
            for k in range(KH):
                last = e.matmul(pst[:, 0:E], xlnT[:, k, ti * 128:ti * 128 + 128],
                                gwb[:, k, :], start=(k == 0), stop=(k == KH - 1))
            return last
        p.step("tensor", reads=[("xlnT", ti), ("gwb", 0)], writes=[("psum", pn, 0)],
               sem=sT, inc=1, f=rt_f)
        p.step("scalar", reads=[("psum", pn, 0)], writes=[("rtr", 0)], sem=sS, inc=1,
               f=lambda e, pst=pst: e.activation(out=rtr[:, 0, 0:E], in_=pst[:, 0:E],
                                                 func=AF.Sigmoid))
        p.step("vector", reads=[("rtr", 0), ("ebs", 0)], writes=[("rtr", 1)], sem=sV, inc=1,
               f=lambda e: e.tensor_add(out=rtr[:, 1, 0:E], in0=rtr[:, 0, 0:E], in1=ebs[:, :]))
        p.step("vector", reads=[("rtr", 1)], writes=[("rtr", 2)], sem=sV, inc=1,
               f=lambda e: e.tensor_copy(out=rtr[:, 2, 0:E], in_=rtr[:, 1, 0:E]))
        for j in range(TOPK - 1):
            p.step("vector", reads=[("rtr", 1)], writes=[("rst", 0)], sem=sV, inc=1,
                   f=lambda e: e.reduce_max(out=rst[:, 0:1], in_=rtr[:, 1, 0:E], axis=AX.X))
            p.step("vector", reads=[("rtr", 1), ("rst", 0)], writes=[("rtr", 3)], sem=sV, inc=1,
                   f=lambda e: e.tensor_scalar(out=rtr[:, 3, 0:E], in0=rtr[:, 1, 0:E],
                                               scalar1=rst[:, 0:1], scalar2=None,
                                               op0=ALU.is_equal))
            p.step("vector", reads=[("rtr", 1), ("rtr", 3)], writes=[("rtr", 1)], sem=sV, inc=1,
                   f=lambda e: e.scalar_tensor_tensor(out=rtr[:, 1, 0:E], in0=rtr[:, 3, 0:E],
                                                      scalar=-1e30, in1=rtr[:, 1, 0:E],
                                                      op0=ALU.mult, op1=ALU.add))
        p.step("vector", reads=[("rtr", 1)], writes=[("rst", 0)], sem=sV, inc=1,
               f=lambda e: e.reduce_max(out=rst[:, 0:1], in_=rtr[:, 1, 0:E], axis=AX.X))
        p.step("vector", reads=[("rtr", 2), ("rst", 0)], writes=[("rtr", 1)], sem=sV, inc=1,
               f=lambda e: e.tensor_scalar(out=rtr[:, 1, 0:E], in0=rtr[:, 2, 0:E],
                                           scalar1=rst[:, 0:1], scalar2=None, op0=ALU.is_ge))
        p.step("vector", reads=[("rtr", 0), ("rtr", 1)], writes=[("rtr", 0)], sem=sV, inc=1,
               f=lambda e: e.tensor_mul(out=rtr[:, 0, 0:E], in0=rtr[:, 0, 0:E],
                                        in1=rtr[:, 1, 0:E]))
        p.step("vector", reads=[("rtr", 0)], writes=[("rst", 1)], sem=sV, inc=1,
               f=lambda e: e.reduce_sum(out=rst[:, 1:2], in_=rtr[:, 0, 0:E], axis=AX.X))
        p.step("vector", reads=[("rst", 1)], writes=[("rst", 2)], sem=sV, inc=1,
               f=lambda e: e.reciprocal(out=rst[:, 2:3], in_=rst[:, 1:2]))
        p.step("vector", reads=[("rtr", 0), ("rst", 2)], writes=[("c", ti)], sem=sV, inc=1,
               f=lambda e, ti=ti: e.tensor_scalar(out=cbuf[:, ti, :], in0=rtr[:, 0, 0:E],
                                                  scalar1=rst[:, 2:3], scalar2=0.5,
                                                  op0=ALU.mult, op1=ALU.mult))

    blk = [0]
    stgi = [0]
    csel = [0]
    qsel = [0]

    def cast_step(eng, dst_ap, src_ap, reads, writes):
        sem = {"vector": sV, "scalar": sS}[eng]
        if eng == "scalar":
            p.step(eng, reads=reads, writes=writes, sem=sem, inc=1,
                   f=lambda e, d=dst_ap, s=src_ap: e.copy(out=d, in_=s))
        else:
            p.step(eng, reads=reads, writes=writes, sem=sem, inc=1,
                   f=lambda e, d=dst_ap, s=src_ap: e.tensor_copy(out=d, in_=s))

    def stage_load(src_f32, gur_slot, key_slot, cache_ap, cache_sem, ckey):
        slot = stgi[0] % 2
        stgi[0] += 1
        for h2 in (0, 1):
            lo = h2 * (KH // 2)
            kst = ("stg", slot, h2)
            dma(("sync", "scalar")[h2], dStg[slot * 2 + h2],
                stg[:, slot, lo * 128:(lo + KH // 2) * 128],
                src_f32[:, lo:lo + KH // 2, :].rearrange("p k j -> p (k j)"),
                writes=[kst])
            cast_step("scalar",
                      gur[:, gur_slot, lo:lo + KH // 2, :],
                      stg[:, slot, lo * 128:(lo + KH // 2) * 128]
                      .rearrange("p (k j) -> p k j", j=128),
                      reads=[kst], writes=[("gur", key_slot, h2)])
        dma("gpsimd", cache_sem, cache_ap, gur[:, gur_slot, :, :],
            reads=[("gur", key_slot, 0), ("gur", key_slot, 1)], writes=[ckey])

    def stream_pair(pair_src, pair_cache, ckbase, pass0):
        slot = blk[0] % NSLOT
        assert slot % 2 == 0
        blk[0] += 2
        ck0, ck1 = ckbase + (0,), ckbase + (1,)
        if pass0:
            for w in (0, 1):
                stage_load(pair_src[w], slot + w, slot + w,
                           pair_cache[w], dCw[slot + w], (ck0, ck1)[w])
        else:
            dma("sync", dGu[slot], gur[:, slot, :, :],
                pair_cache[0], reads=[ck0], writes=[("gur", slot, 0), ("gur", slot, 1)])
            dma("scalar", dGu[slot + 1], gur[:, slot + 1, :, :],
                pair_cache[1], reads=[ck1],
                writes=[("gur", slot + 1, 0), ("gur", slot + 1, 1)])
        return slot, slot + 1

    wsti = [0]

    def stream_wd_chunks(e_idx, k2):
        for k in (k2, k2 + 1):
            slot = wsti[0] % 2
            wsti[0] += 1
            kst = ("wdstg", slot)
            dma("gpsimd", dStgW[slot], wdstg[:, slot, :], wdd[e_idx, k], writes=[kst])
            cast_step("scalar", wdsb[:, k, :], wdstg[:, slot, :],
                      reads=[kst], writes=[("wd", k)])
            dma("gpsimd", dCwd[k], cwd[e_idx, k], wdsb[:, k, :],
                reads=[("wd", k)], writes=[("cwd", e_idx, k)])

    XLK = [("xlnT", t2) for t2 in range(GT)]

    def mm_dmaj(slot, pn):
        ps_t = PS[pn]

        def f(e, slot=slot, ps_t=ps_t):
            last = None
            for k in range(KH):
                for (s0, w) in TCH:
                    last = e.matmul(ps_t[:, s0:s0 + w], gur[:, slot, k, :],
                                    xlnT[:, k, s0:s0 + w],
                                    start=(k == 0), stop=(k == KH - 1))
            return last
        p.step("tensor", reads=[("gur", slot, 0), ("gur", slot, 1)] + XLK,
               writes=[("psum", pn, s0 // 512) for (s0, _) in TCH],
               sem=sT, inc=1, f=f)

    def ffn_call(src_ap, src_fam, dst_ap, dst_fam, pass0_call, resident_g0=False):
        pref = {}
        for g in range(NG):
            pass0 = pass0_call and g == 0
            ln_group(g, src_ap, src_fam, resident=(resident_g0 and g == 0))
            for e_idx in range(E + 1):
                if not pass0:
                    for j, k2 in enumerate(range(0, KI, 2)):
                        dma("gpsimd", dWd[k2], wdsb[:, k2:k2 + 2, :],
                            cwd[e_idx, k2:k2 + 2].rearrange("w p n -> p w n"),
                            reads=[("cwd", e_idx, k2), ("cwd", e_idx, k2 + 1)],
                            writes=[("wd", k2), ("wd", k2 + 1)])
                def act_m(m):
                    pg, pu = ("a", "b") if m % 2 == 0 else ("c", "d")
                    ss = m % 2
                    for hh in (0, 1):
                        lo = hh * 512
                        p.step("scalar", reads=[("psum", pg, hh)],
                               writes=[("sgh", ss, hh)], sem=sS, inc=1,
                               f=lambda e, ss=ss, pg=pg, lo=lo: e.activation(
                                   out=sg[:, ss, lo:lo + 512],
                                   in_=PS[pg][:, lo:lo + 512], func=AF.Silu))
                        p.step("vector",
                               reads=[("sgh", ss, hh), ("psum", pu, hh)],
                               writes=[("hp", m, hh)], sem=sV, inc=1,
                               f=lambda e, ss=ss, m=m, pu=pu, lo=lo: e.tensor_mul(
                                   out=hp[:, m, lo:lo + 512],
                                   in0=sg[:, ss, lo:lo + 512],
                                   in1=PS[pu][:, lo:lo + 512]))
                for m in range(MI):
                    if (g, e_idx, m) in pref:
                        gslot, uslot = pref.pop((g, e_idx, m))
                    else:
                        gslot, uslot = stream_pair(wgu[e_idx, m], cgu[e_idx, m],
                                                   ("cgu", e_idx, m), pass0)
                    if m >= 1:
                        act_m(m - 1)
                    pg, pu = ("a", "b") if m % 2 == 0 else ("c", "d")
                    mm_dmaj(gslot, pg)
                    mm_dmaj(uslot, pu)
                    if pass0 and 1 <= m <= 4:
                        stream_wd_chunks(e_idx, (m - 1) * 2)
                act_m(MI - 1)
                nxt = (g, e_idx + 1) if e_idx < E else \
                      ((g + 1, 0) if g + 1 < NG else None)
                if nxt is not None:
                    ng, ne = nxt
                    np0 = pass0_call and ng == 0
                    for m2 in (0, 1):
                        pref[(ng, ne, m2)] = stream_pair(
                            wgu[ne, m2], cgu[ne, m2], ("cgu", ne, m2), np0)
                for ti in range(GT):
                    pr = ("a", "b") if ti % 2 == 0 else ("c", "d")
                    q0, q1 = PS[pr[0]], PS[pr[1]]

                    def ph2_f(e, ti=ti, q0=q0, q1=q1):
                        last = None
                        for k2 in range(KI):
                            for (s0, w) in HCH:
                                dst = q0 if s0 < 1024 else q1
                                ds0 = s0 % 1024
                                last = e.matmul(dst[:, ds0:ds0 + w],
                                                hp[:, k2, ti * 128:ti * 128 + 128],
                                                wdsb[:, k2, s0:s0 + w],
                                                start=(k2 == 0), stop=(k2 == KI - 1))
                        return last
                    dslots = sorted({("psum", pr[0] if s0 < 1024 else pr[1],
                                      (s0 % 1024) // 512) for (s0, _) in HCH})
                    p.step("tensor",
                           reads=[("hp", k2, ti // 4) for k2 in range(KI)]
                           + [("wd", k2) for k2 in range(KI)],
                           writes=dslots, sem=sT, inc=1, f=ph2_f)
                    for qi, qn in enumerate(pr):
                        lo = qi * 1024
                        if lo >= H:
                            break
                        w = min(1024, H - lo)
                        rd = [("psum", qn, 0)] + ([("psum", qn, 1)] if w > 512 else [])
                        if e_idx < E:
                            rd.append(("c", ti))
                        p.step("vector", reads=rd, writes=[("xg", ti)], sem=sV, inc=1,
                               f=lambda e, q=PS[qn], lo=lo, w=w, ti=ti, ee=e_idx:
                               e.scalar_tensor_tensor(
                                   out=xg[:, ti, lo:lo + w], in0=q[:, 0:w],
                                   scalar=(cbuf[:, ti, ee:ee + 1] if ee < E else 0.5),
                                   in1=xg[:, ti, lo:lo + w], op0=ALU.mult, op1=ALU.add))
            if pass0:
                for k2 in range(KI):
                    p.join_keys([("cwd", e2, k2) for e2 in range(E + 1)], dCwd[k2])
            for ti in range(GT):
                tt = g * GT + ti
                dma("scalar", dXW[ti], dst_ap[tt * 128:tt * 128 + 128, :], xg[:, ti, :],
                    reads=[("xg", ti)], writes=[(dst_fam, tt)])

    def griffin():
        p.step("vector", writes=[("zb", 0)], sem=sV, inc=1,
               f=lambda e: e.memset(zb[:, 0:TG], 0.0))

        def chain_m(m, g):
            r = m % 2
            pa, pb = ("a", "b") if m % 2 == 0 else ("c", "d")
            kpa = [("psum", pa, 0), ("psum", pa, 1)]
            kpb = [("psum", pb, 0), ("psum", pb, 1)]
            al, xb_ = ("gfa", r), ("gfa", 2 + r)
            hh, AA = ("gfa", 4 + r), ("gfa", 6 + r)
            p.step("scalar", reads=kpb + [("bgbh", 0)], writes=[("gsc", r)],
                   sem=sS, inc=1,
                   f=lambda e, r=r, m=m, pb=pb: e.activation(
                       out=gsc[:, r, :], in_=PS[pb][:, 0:TG],
                       func=AF.Tanh, scale=0.5, bias=bgbh[:, m:m + 1]))
            p.step("scalar", reads=kpa + [("binbh", 0)], writes=[("sg", r)],
                   sem=sS, inc=1,
                   f=lambda e, r=r, m=m, pa=pa: e.activation(
                       out=sg[:, r, :], in_=PS[pa][:, 0:TG],
                       func=AF.Tanh, scale=0.5, bias=binbh[:, m:m + 1]))
            p.step("vector", reads=[("gsc", r), ("sdmh", 0)], writes=[("gsc", 2 + r)],
                   sem=sV, inc=1,
                   f=lambda e, r=r, m=m: e.tensor_scalar(
                       out=gsc[:, 2 + r, :], in0=gsc[:, r, :],
                       scalar1=sdmh[:, m:m + 1], scalar2=sdmh[:, m:m + 1],
                       op0=ALU.mult, op1=ALU.add))
            p.step("scalar", reads=[("gsc", 2 + r)], writes=[al], sem=sS, inc=1,
                   f=lambda e, r=r: e.activation(out=gfa[:, r, :], in_=gsc[:, 2 + r, :],
                                                 func=AF.Exp, scale=-1.0))
            p.step("vector", reads=[al], writes=[("gsc", 2 + r)], sem=sV, inc=1,
                   f=lambda e, r=r: e.tensor_mul(out=gsc[:, 2 + r, :],
                                                 in0=gfa[:, r, :], in1=gfa[:, r, :]))
            p.step("scalar", reads=[("gsc", 2 + r), ("onec", 0)],
                   writes=[("gsc", 2 + r)], sem=sS, inc=1,
                   f=lambda e, r=r: e.activation(out=gsc[:, 2 + r, :],
                                                 in_=gsc[:, 2 + r, :], func=AF.Ln,
                                                 scale=-1.0, bias=onec[:, 0:1]))
            p.step("scalar", reads=[("gsc", 2 + r)], writes=[("gsc", 2 + r)],
                   sem=sS, inc=1,
                   f=lambda e, r=r: e.activation(out=gsc[:, 2 + r, :],
                                                 in_=gsc[:, 2 + r, :], func=AF.Exp,
                                                 scale=0.5))
            p.step("vector", reads=[("sg", r)], writes=[("sg", r)], sem=sV, inc=1,
                   f=lambda e, r=r: e.tensor_scalar(
                       out=sg[:, r, :], in0=sg[:, r, :], scalar1=0.5, scalar2=0.5,
                       op0=ALU.mult, op1=ALU.add))
            p.step("vector", reads=[("sg", r)] + XLK, writes=[xb_], sem=sV, inc=1,
                   f=lambda e, r=r, m=m: e.tensor_mul(out=gfa[:, 2 + r, :],
                                                      in0=sg[:, r, :],
                                                      in1=xlnT[:, m, :]))
            p.step("vector", reads=[xb_, ("gsc", 2 + r)], writes=[xb_], sem=sV, inc=1,
                   f=lambda e, r=r: e.tensor_mul(out=gfa[:, 2 + r, :],
                                                 in0=gfa[:, 2 + r, :],
                                                 in1=gsc[:, 2 + r, :]))
            p.step("vector", reads=[al, xb_] + ([("hend", m)] if g else []),
                   writes=[hh], sem=sV, inc=1,
                   f=lambda e, r=r, m=m, g=g: e.tensor_tensor_scan(
                       out=gfa[:, 4 + r, :], data0=gfa[:, r, :],
                       data1=gfa[:, 2 + r, :],
                       initial=(0.0 if g == 0 else hend[:, m:m + 1]),
                       op0=ALU.mult, op1=ALU.add))
            p.step("vector", reads=[al, ("zb", 0)] + ([("aend", m)] if g else []),
                   writes=[AA], sem=sV, inc=1,
                   f=lambda e, r=r, m=m, g=g: e.tensor_tensor_scan(
                       out=gfa[:, 6 + r, :], data0=gfa[:, r, :], data1=zb[:, 0:TG],
                       initial=(1.0 if g == 0 else aend[:, m:m + 1]),
                       op0=ALU.mult, op1=ALU.add))
            p.step("vector", reads=[hh], writes=[("hend", m)], sem=sV, inc=1,
                   f=lambda e, r=r, m=m: e.tensor_copy(out=hend[:, m:m + 1],
                                                       in_=gfa[:, 4 + r, TG - 1:TG]))
            p.step("vector", reads=[AA], writes=[("aend", m)], sem=sV, inc=1,
                   f=lambda e, r=r, m=m: e.tensor_copy(out=aend[:, m:m + 1],
                                                       in_=gfa[:, 6 + r, TG - 1:TG]))
            dma("gpsimd", dHdH[r], hdram[m, :, g * TG:(g + 1) * TG], gfa[:, 4 + r, :],
                reads=[hh], writes=[("hd", m, g)])
            dma("scalar", dHdA[r], adram[m, :, g * TG:(g + 1) * TG], gfa[:, 6 + r, :],
                reads=[AA], writes=[("ad", m, g)])

        for g in range(NG):
            pass0 = g == 0
            ln_group(g, xr, "xr", with_router=False)
            prev = None
            for m in range(KH):
                islot, gslot = stream_pair(wio[m], cio[m], ("cio", m), pass0)
                if prev is not None:
                    chain_m(prev, g)
                pa, pb = ("a", "b") if m % 2 == 0 else ("c", "d")
                mm_dmaj(islot, pa)
                mm_dmaj(gslot, pb)
                prev = m
            chain_m(KH - 1, g)
        for r in range(2):
            p.join_keys([("hd", m, g) for m in range(KH) for g in range(NG) if m % 2 == r],
                        dHdH[r])
            p.join_keys([("ad", m, g) for m in range(KH) for g in range(NG) if m % 2 == r],
                        dHdA[r])

        def xchg_f(e):
            with e.register("hreg") as hreg, e.register("cur") as cur, \
                 e.register("tgt") as tgt, e.register("cnt") as cnt, \
                 e.register("go") as go:
                e.reg_load(hreg, hsb[0:1, 0:1])
                myslot = e.snap(hreg, min_val=0, max_val=1)
                e.dma_start(out=shd[bass.ds(myslot, 1), :, :],
                            in_=hend[:, :]).then_inc(dPoll, 16)
                e.wait_ge(dPoll, 16)
                e.dma_start(out=shf[bass.ds(myslot, 1), :, :],
                            in_=nsb[:, 0:1]).then_inc(dPoll, 16)
                e.wait_ge(dPoll, 32)
                e.reg_load(tgt, nsb[0:1, 0:1])
                e.reg_mov(cnt, 32)
                e.reg_mov(go, 1)
                with e.While(go):
                    e.dma_start(out=fsb[:, 0:1],
                                in_=shf[0:1, :, :]).then_inc(dPoll, 16)
                    e.reg_add(cnt, cnt, 16)
                    e.wait_ge(dPoll, cnt)
                    e.reg_load(cur, fsb[0:1, 0:1])
                    e.reg_alu(go, cur, tgt, ALU.not_equal)
                return e.dma_start(out=ccrs[:, :], in_=shd[0:1, :, :])
        for m in (0, 1):
            r = m % 2
            dma("sync", dHb[r], gfb[:, r, :], hdram[m, :, TG:2 * TG],
                reads=[("hd", m, 1)], writes=[("gfb", r)])
            dma("scalar", dAb[r], gfb[:, 2 + r, :], adram[m, :, TG:2 * TG],
                reads=[("ad", m, 1)], writes=[("gfb", 2 + r)])
        p.step("gpsimd", reads=[("hend", m) for m in range(KH)]
               + [("nsb", 0), ("hsb", 0)],
               writes=[("ccrs", 0)], sem=dCc, inc=16, f=xchg_f)
        p.step("vector", reads=[("ccrs", 0), ("flg", 0)], writes=[("ceff", 0)], sem=sV, inc=1,
               f=lambda e: e.tensor_scalar_mul(out=ceff[:, :], in0=ccrs[:, :],
                                               scalar1=flg[:, 0:1]))

        for g in (1, 0):
            if g == 0:
                for ti in range(GT):
                    tt = g * GT + ti
                    dma("gpsimd", dX[ti], xg[:, ti, :], xr[tt * 128:tt * 128 + 128, :],
                        reads=[("xr", tt)], writes=[("xg", ti)])
            for m in range(KH):
                r = m % 2
                kh_, ka = ("gfb", r), ("gfb", 2 + r)
                if not (g == 1 and m < 2):
                    dma("sync", dHb[r], gfb[:, r, :], hdram[m, :, g * TG:(g + 1) * TG],
                        reads=[("hd", m, g)], writes=[kh_])
                    dma("scalar", dAb[r], gfb[:, 2 + r, :],
                        adram[m, :, g * TG:(g + 1) * TG],
                        reads=[("ad", m, g)], writes=[ka])
                kht = ("ght", r)
                p.step("vector", reads=[kh_, ka, ("ceff", 0)], writes=[kht], sem=sV, inc=1,
                       f=lambda e, r=r, m=m: e.scalar_tensor_tensor(
                           out=ght[:, r, :], in0=gfb[:, 2 + r, :], scalar=ceff[:, m:m + 1],
                           in1=gfb[:, r, :], op0=ALU.mult, op1=ALU.add))
                pn = "c" if r == 0 else "d"
                pst = PS[pn]
                psk = [("psum", pn, 0), ("psum", pn, 1)]

                def htr_f(e, r=r, pst=pst):
                    last = None
                    for t2 in range(GT):
                        last = e.transpose(pst[:, t2 * 128:t2 * 128 + 128],
                                           ght[:, r, t2 * 128:t2 * 128 + 128], idf[:, :])
                    return last
                p.step("tensor", reads=[kht, ("idf", 0)], writes=psk, sem=sT, inc=1, f=htr_f)
                xgk = [("xg", t2) for t2 in range(GT)]
                p.step("vector", reads=psk + xgk, writes=xgk, sem=sV, inc=1,
                       f=lambda e, m=m, pst=pst: e.tensor_add(
                           out=xg[:, :, m * 128:m * 128 + 128],
                           in0=xg[:, :, m * 128:m * 128 + 128],
                           in1=pst[:, 0:GT * 128].rearrange("p (t j) -> p t j", j=128)))
            for ti in range(GT):
                tt = g * GT + ti
                dma("scalar", dXW[ti], xr[tt * 128:tt * 128 + 128, :], xg[:, ti, :],
                    reads=[("xg", ti)], writes=[("xr", tt)])

    ffn_call(x_in, None, xr, "xr", pass0_call=True)
    griffin()
    ffn_call(xr, "xr", y_out, "yd", pass0_call=False, resident_g0=True)

    for t in range(GT):
        p.wait_abs("gpsimd", dXW[t], dXW[t].n)

    with st:
        block = nc.Block()
        with block:
            p.emit(block)
    return nc



_CACHE = {}
_NONCE = [int(np.random.SeedSequence().entropy % 2000000000) + 1]


def kernel(L0_attn_x, L0_attn_W_in, L0_attn_b_in, L0_attn_W_gate, L0_attn_b_gate,
           L0_attn_forget_lambda, L0_ffn_gate_w, L0_ffn_expert_bias, L0_ffn_w_gate,
           L0_ffn_w_up, L0_ffn_w_down, L0_ffn_sw_gate, L0_ffn_sw_up, L0_ffn_sw_down):
    cfg = derived(full_cfg())
    H, E, KH, TLOC = cfg["H"], cfg["E"], cfg["KH"], cfg["TLOC"]
    B = 4

    x = np.ascontiguousarray(np.asarray(L0_attn_x, dtype=np.float32))
    arr_gu, arr_wd, arr_io, arr_gw = pack_weights(
        cfg,
        np.asarray(L0_ffn_w_gate, np.float32), np.asarray(L0_ffn_w_up, np.float32),
        np.asarray(L0_ffn_w_down, np.float32), np.asarray(L0_ffn_sw_gate, np.float32),
        np.asarray(L0_ffn_sw_up, np.float32), np.asarray(L0_ffn_sw_down, np.float32),
        np.asarray(L0_attn_W_in, np.float32), np.asarray(L0_attn_W_gate, np.float32),
        np.asarray(L0_ffn_gate_w, np.float32))
    eb = np.ascontiguousarray(np.broadcast_to(
        np.asarray(L0_ffn_expert_bias, np.float32), (128, E)))
    bin_a = np.ascontiguousarray(np.asarray(L0_attn_b_in, np.float32).reshape(KH, 128).T)
    bg_a = np.ascontiguousarray(np.asarray(L0_attn_b_gate, np.float32).reshape(KH, 128).T)
    lam_a = np.ascontiguousarray(
        np.asarray(L0_attn_forget_lambda, np.float32).reshape(KH, 128).T)
    identf = np.eye(128, dtype=np.float32)

    if "nc" not in _CACHE:
        _CACHE["nc"] = build_program(full_cfg())
    nc = _CACHE["nc"]
    _NONCE[0] = (_NONCE[0] * 1103515245 + 12345) % 2147483629 + 1

    xs = x.reshape(B, 2, TLOC, H)
    in_maps = []
    for c in range(8):
        b, half = c // 2, c % 2
        in_maps.append(dict(
            x=np.ascontiguousarray(xs[b, half]), wgu=arr_gu, wd=arr_wd, wio=arr_io,
            gw=arr_gw, ebias=eb, bin=bin_a, bgate=bg_a, lam=lam_a,
            flag=np.full((128, 1), float(half), np.float32),
            nflag=np.full((128, 1), 1.0 - float(half), np.float32),
            identf=identf,
            onec=np.full((128, 1), 1.0 + 1e-6, np.float32),
            nonce=np.array([[_NONCE[0]]], np.int32),
            halfi=np.array([[half]], np.int32)))
    res = run_bass_kernel_spmd(nc, in_maps, core_ids=list(range(8)))
    out = np.empty((B, 4096, H), np.float32)
    for c in range(8):
        b, half = c // 2, c % 2
        out[b, half * TLOC:(half + 1) * TLOC, :] = res.results[c]["y"]
    return out


# version 25 (final)
# speedup vs baseline: 1.0847x; 1.0207x over previous
import numpy as np
import ml_dtypes
import concourse.bass as bass
import concourse.mybir as mybir
from concourse.bass_utils import run_bass_kernel_spmd
from contextlib import ExitStack

F32 = mybir.dt.float32
I32 = mybir.dt.int32
BF16 = mybir.dt.bfloat16
AF = mybir.ActivationFunctionType
ALU = mybir.AluOpType
AX = mybir.AxisListType


def full_cfg():
    return dict(H=2048, I=1024, E=16, TOPK=4, GT=8, NG=2,
                PAIRS=[[0, 1], [2, 3], [4, 5], [6, 7]])


def derived(cfg):
    d = dict(cfg)
    d["KH"] = cfg["H"] // 128
    d["KI"] = cfg["I"] // 128
    d["TG"] = cfg["GT"] * 128
    d["TILES"] = cfg["GT"] * cfg["NG"]
    d["TLOC"] = d["TILES"] * 128
    d["NSLOT"] = 4
    return d



def pack_weights(cfg, wg, wu, wd, sw_g, sw_u, sw_d, W_in, W_gate, gate_w):
    H, I, E, KH, KI = cfg["H"], cfg["I"], cfg["E"], cfg["KH"], cfg["KI"]
    MI = KI
    wg_all = np.concatenate([wg, sw_g[None]], 0)
    wu_all = np.concatenate([wu, sw_u[None]], 0)
    wd_all = np.concatenate([wd, sw_d[None]], 0)

    def gu(a):
        return a.reshape(E + 1, MI, 128, KH, 128).transpose(0, 1, 4, 3, 2)

    arr_gu = np.ascontiguousarray(
        np.stack([gu(wg_all), gu(wu_all)], axis=2))
    arr_wd = np.ascontiguousarray(
        wd_all.reshape(E + 1, H, KI, 128).transpose(0, 2, 3, 1))

    def io(W):
        return W.reshape(KH, 128, KH, 128).transpose(0, 3, 2, 1)

    arr_io = np.ascontiguousarray(
        np.stack([io(W_in), io(W_gate)], axis=1))
    arr_gw = np.ascontiguousarray(gate_w.T.reshape(KH, 128, E))
    return arr_gu, arr_wd, arr_io, arr_gw



ENGS = ("tensor", "vector", "scalar", "gpsimd", "sync")


class Sem:
    __slots__ = ("h", "n", "name")

    def __init__(self, h, name):
        self.h = h
        self.n = 0
        self.name = name


class Plan:

    def __init__(self):
        self.ops = {e: [] for e in ENGS}
        self.last_w = {}
        self.readers = {}
        self.hw = {e: {} for e in ENGS}
        self.region_of = {}
        self.region_keys = {}

    def _cands(self, key):
        fam = key[0]
        out = [key]
        reg = self.region_of.get(fam)
        if reg is not None:
            for f2, ks in self.region_keys.get(reg, {}).items():
                if f2 != fam:
                    out.extend(ks)
        return out

    def _register(self, key):
        fam = key[0]
        reg = self.region_of.get(fam)
        if reg is not None:
            self.region_keys.setdefault(reg, {}).setdefault(fam, set()).add(key)

    def step(self, eng, reads=(), writes=(), sem=None, inc=0, f=None):
        need = {}

        def add(marks):
            for (s, v) in marks:
                if v > need.get(s, 0):
                    need[s] = v
        for k in reads:
            for c in self._cands(k):
                if c in self.last_w:
                    add([self.last_w[c]])
        for k in writes:
            for c in self._cands(k):
                if c in self.last_w:
                    add([self.last_w[c]])
                if c in self.readers:
                    add(self.readers[c])
        hw = self.hw[eng]
        waits = []
        for s, v in sorted(need.items(), key=lambda kv: kv[0].name):
            if hw.get(s.name, 0) < v:
                waits.append((s, v))
                hw[s.name] = v
        if sem is not None:
            sem.n += inc
            marker = (sem, sem.n)
        else:
            marker = None
        self.ops[eng].append((waits, f, sem, inc))
        if marker is not None:
            for k in writes:
                self._register(k)
                self.last_w[k] = marker
                self.readers[k] = []
            for k in reads:
                self._register(k)
                self.readers.setdefault(k, []).append(marker)
        else:
            assert not writes
        return marker

    def join_keys(self, keys, sem):
        for k in keys:
            if k in self.last_w and self.last_w[k][0] is sem:
                self.last_w[k] = (sem, sem.n)

    def wait_abs(self, eng, sem, val):
        hw = self.hw[eng]
        if hw.get(sem.name, 0) < val:
            hw[sem.name] = val
            self.ops[eng].append(([(sem, val)], None, None, 0))

    def emit(self, block):
        def runner(lst):
            def run(engine):
                for waits, f, sem, inc in lst:
                    for (s, v) in waits:
                        engine.wait_ge(s.h, v)
                    if f is not None:
                        instr = f(engine)
                        if sem is not None:
                            instr.then_inc(sem.h, inc)
            return run
        block.sync(runner(self.ops["sync"]))
        block.tensor(runner(self.ops["tensor"]))
        block.vector(runner(self.ops["vector"]))
        block.scalar(runner(self.ops["scalar"]))
        block.gpsimd(runner(self.ops["gpsimd"]))



def build_program(cfg, detect_races=False):
    cfg = derived(cfg)
    H, I, E, KH, KI = cfg["H"], cfg["I"], cfg["E"], cfg["KH"], cfg["KI"]
    MI, GT, NG, TG = KI, cfg["GT"], cfg["NG"], cfg["TG"]
    TILES, TLOC, NSLOT = cfg["TILES"], cfg["TLOC"], cfg["NSLOT"]
    TOPK = cfg["TOPK"]
    TCH = [(s, min(512, TG - s)) for s in range(0, TG, 512)]
    HCH = [(s, min(512, H - s)) for s in range(0, H, 512)]
    EPS_LN = 1e-5

    nc = bass.Bass("TRN2", target_bir_lowering=False, debug=False,
                   detect_race_conditions=detect_races)
    arena = nc.alloc_sbuf_tensor("arena0", [128, 206 * 1024], mybir.dt.uint8)
    ABASE = nc.lookup_mloc(arena).addr

    x_in = nc.dram_tensor("x", [TLOC, H], F32, kind="ExternalInput").ap()
    wgu = nc.dram_tensor("wgu", [E + 1, MI, 2, 128, KH, 128], F32, kind="ExternalInput").ap()
    wdd = nc.dram_tensor("wd", [E + 1, KI, 128, H], F32, kind="ExternalInput").ap()
    wio = nc.dram_tensor("wio", [KH, 2, 128, KH, 128], F32, kind="ExternalInput").ap()
    gw = nc.dram_tensor("gw", [KH, 128, E], F32, kind="ExternalInput").ap()
    ebias = nc.dram_tensor("ebias", [128, E], F32, kind="ExternalInput").ap()
    bin_d = nc.dram_tensor("bin", [128, KH], F32, kind="ExternalInput").ap()
    bgate_d = nc.dram_tensor("bgate", [128, KH], F32, kind="ExternalInput").ap()
    lam_d = nc.dram_tensor("lam", [128, KH], F32, kind="ExternalInput").ap()
    flag_d = nc.dram_tensor("flag", [128, 1], F32, kind="ExternalInput").ap()
    nflag_d = nc.dram_tensor("nflag", [128, 1], F32, kind="ExternalInput").ap()
    ident_d = nc.dram_tensor("identf", [128, 128], F32, kind="ExternalInput").ap()
    y_out = nc.dram_tensor("y", [TLOC, H], F32, kind="ExternalOutput").ap()

    xr = nc.dram_tensor("xr", [TLOC, H], F32).ap()
    cgu = nc.dram_tensor("cgu", [E + 1, MI, 2, 128, KH, 128], BF16).ap()
    cwd = nc.dram_tensor("cwd", [E + 1, KI, 128, H], BF16).ap()
    cio = nc.dram_tensor("cio", [KH, 2, 128, KH, 128], BF16).ap()
    hdram = nc.dram_tensor("hdram", [KH, 128, TLOC], F32).ap()
    adram = nc.dram_tensor("adram", [KH, 128, TLOC], F32).ap()
    shd = nc.dram_tensor("shd", [2, 128, KH], F32, addr_space="Shared").ap()
    shf = nc.dram_tensor("shf", [2, 1, 1], I32, addr_space="Shared").ap()
    onec_d = nc.dram_tensor("onec", [128, 1], F32, kind="ExternalInput").ap()
    nonce_d = nc.dram_tensor("nonce", [1, 1], I32, kind="ExternalInput").ap()
    half_d = nc.dram_tensor("halfi", [1, 1], I32, kind="ExternalInput").ap()

    st = ExitStack()
    EE = st.enter_context

    off = [0]

    def alloc(name, shape, dtype, at=None):
        per = int(np.prod(shape[1:])) * mybir.dt.size(dtype)
        if at is None:
            at = off[0]
            off[0] = at + ((per + 127) // 128) * 128
        t = nc.alloc_sbuf_tensor_at(name, list(shape), dtype, offset=ABASE + at)
        return t, at

    def region(n_bytes):
        at = off[0]
        off[0] = at + ((n_bytes + 127) // 128) * 128
        return at

    xg, _ = alloc("xg", [128, GT, H], F32)
    xlnT, _ = alloc("xlnT", [128, KH, TG], BF16)
    r_wd = region(max(KI * H * 2, 8 * TG * 4, 4 * H * 4))
    wdsb, _ = alloc("wdsb", [128, KI, H], BF16, at=r_wd)
    gfa, _ = alloc("gfa", [128, 8, TG], F32, at=r_wd)
    ydmp, _ = alloc("ydmp", [128, 4, H], F32, at=r_wd)
    r_hp = region(max(MI * TG * 2, 4 * TG * 4))
    hp, _ = alloc("hp", [128, MI, TG], BF16, at=r_hp)
    gfb, _ = alloc("gfb", [128, 4, TG], F32, at=r_hp)
    zb, _ = alloc("zb", [128, TG], F32, at=r_hp)
    gur, _ = alloc("gur", [128, NSLOT, KH, 128], BF16)
    stg, _ = alloc("stg", [128, 2, H], F32)
    r_yf = region(max(2 * H * 4, 4 * TG * 4))
    yf, _ = alloc("yf", [128, 2, H], F32, at=r_yf)
    wdstg, _ = alloc("wdstg", [128, 2, H], F32, at=r_yf)
    ght, _ = alloc("ght", [128, 2, TG], F32, at=r_yf)
    gsc, _ = alloc("gsc", [128, 4, TG], F32, at=r_yf)
    sg, _ = alloc("sg", [128, 2, TG], F32)
    cbuf, _ = alloc("cbuf", [128, GT, E], F32)
    stats, _ = alloc("stats", [128, GT, 8], F32)
    rtr, _ = alloc("rtr", [128, 4, max(E, 4)], F32)
    rst, _ = alloc("rst", [128, 4], F32)
    gwb, _ = alloc("gwb", [128, KH, E], BF16)
    gwf, _ = alloc("gwf", [128, KH, E], F32)
    ebs, _ = alloc("ebs", [128, E], F32)
    binb, _ = alloc("binb", [128, KH], F32)
    bgb, _ = alloc("bgb", [128, KH], F32)
    lamb, _ = alloc("lamb", [128, KH], F32)
    sdm, _ = alloc("sdm", [128, KH], F32)
    sdmh, _ = alloc("sdmh", [128, KH], F32)
    bgbh, _ = alloc("bgbh", [128, KH], F32)
    binbh, _ = alloc("binbh", [128, KH], F32)
    tA, _ = alloc("tA", [128, KH], F32)
    tB, _ = alloc("tB", [128, KH], F32)
    flg, _ = alloc("flg", [128, 1], F32)
    onec, _ = alloc("onec", [128, 1], F32)
    nflg, _ = alloc("nflg", [128, 1], F32)
    idf, _ = alloc("idf", [128, 128], F32)
    hend, _ = alloc("hend", [128, KH], F32)
    aend, _ = alloc("aend", [128, KH], F32)
    ceff, _ = alloc("ceff", [128, KH], F32)
    ccrs, _ = alloc("ccrs", [128, KH], F32)
    nsb, _ = alloc("nsb", [1, 4], I32)
    hsb, _ = alloc("hsb", [1, 4], I32)
    fsb, _ = alloc("fsb", [1, 4], I32)
    assert off[0] <= 206 * 1024, f"sbuf over budget: {off[0]/1024:.1f} KiB"

    ps_a = EE(nc.psum_tensor("ps_a", [128, 1024], F32))
    ps_b = EE(nc.psum_tensor("ps_b", [128, 1024], F32))
    ps_c = EE(nc.psum_tensor("ps_c", [128, 1024], F32))
    ps_d = EE(nc.psum_tensor("ps_d", [128, 1024], F32))
    PS = {"a": ps_a, "b": ps_b, "c": ps_c, "d": ps_d}

    p = Plan()
    p.region_of.update({
        "wd": "R_wd", "gfa": "R_wd", "ydmp": "R_wd",
        "hp": "R_hp", "gfb": "R_hp", "zb": "R_hp",
        "yf": "R_yf", "ght": "R_yf", "gsc": "R_yf", "wdstg": "R_yf",
        "sg": "R_sg", "sgh": "R_sg",
    })

    def mksem(name):
        return Sem(EE(nc.semaphore(name)), name)

    sT, sV, sS, sG = mksem("sT"), mksem("sV"), mksem("sS"), mksem("sG")
    dCst = mksem("dCst")
    dX = [mksem(f"dX{t}") for t in range(GT)]
    dXW = [mksem(f"dXW{t}") for t in range(GT)]
    dGu = [mksem(f"dGu{s}") for s in range(NSLOT)]
    dCw = [mksem(f"dCw{s}") for s in range(NSLOT)]
    dStg = [mksem(f"dStg{s}") for s in range(4)]
    dStgW = [mksem(f"dStgW{s}") for s in range(2)]
    dWd = [mksem(f"dWd{k}") for k in range(KI)]
    dCwd = [mksem(f"dCwd{k}") for k in range(KI)]
    dHdH = [mksem(f"dHdH{r}") for r in range(2)]
    dHdA = [mksem(f"dHdA{r}") for r in range(2)]
    dHb = [mksem(f"dHb{s}") for s in range(2)]
    dAb = [mksem(f"dAb{s}") for s in range(2)]
    dCc = mksem("dCc")
    dPoll = EE(nc.semaphore("dPoll"))

    def dma(eng, sem, out_ap, in_ap, reads=(), writes=()):
        return p.step(eng, reads=reads, writes=writes, sem=sem, inc=16,
                      f=lambda e, o=out_ap, i=in_ap: e.dma_start(out=o, in_=i))

    const_keys = [("gwf", 0), ("ebs", 0), ("binb", 0), ("bgb", 0), ("lamb", 0),
                  ("flg", 0), ("nflg", 0), ("idf", 0), ("onec", 0)]
    dma("sync", dCst, gwf[:, :, :], gw.rearrange("k p e -> p k e"), writes=[("gwf", 0)])
    dma("sync", dCst, ebs[:, :], ebias, writes=[("ebs", 0)])
    dma("sync", dCst, binb[:, :], bin_d, writes=[("binb", 0)])
    dma("sync", dCst, bgb[:, :], bgate_d, writes=[("bgb", 0)])
    dma("sync", dCst, lamb[:, :], lam_d, writes=[("lamb", 0)])
    dma("sync", dCst, flg[:, :], flag_d, writes=[("flg", 0)])
    dma("sync", dCst, onec[:, :], onec_d, writes=[("onec", 0)])
    dma("sync", dCst, nflg[:, :], nflag_d, writes=[("nflg", 0)])
    dma("sync", dCst, idf[:, :], ident_d, writes=[("idf", 0)])
    dma("sync", dCst, nsb[:, 0:1], nonce_d[0], writes=[("nsb", 0)])
    dma("sync", dCst, hsb[:, 0:1], half_d[0], writes=[("hsb", 0)])
    const_keys.extend([("nsb", 0), ("hsb", 0)])
    p.join_keys(const_keys, dCst)

    p.step("vector", reads=[("gwf", 0)], writes=[("gwb", 0)], sem=sV, inc=1,
           f=lambda e: e.tensor_copy(out=gwb[:, :, :], in_=gwf[:, :, :]))
    p.step("scalar", reads=[("lamb", 0)], writes=[("tA", 0)], sem=sS, inc=1,
           f=lambda e: e.activation(out=tA[:, :], in_=lamb[:, :], func=AF.Abs))
    p.step("scalar", reads=[("tA", 0)], writes=[("tA", 0)], sem=sS, inc=1,
           f=lambda e: e.activation(out=tA[:, :], in_=tA[:, :], func=AF.Exp, scale=-1.0))
    p.step("vector", reads=[("tA", 0)], writes=[("tA", 0)], sem=sV, inc=1,
           f=lambda e: e.tensor_scalar_add(out=tA[:, :], in0=tA[:, :], scalar1=1.0))
    p.step("scalar", reads=[("tA", 0)], writes=[("tA", 0)], sem=sS, inc=1,
           f=lambda e: e.activation(out=tA[:, :], in_=tA[:, :], func=AF.Ln))
    p.step("vector", reads=[("lamb", 0)], writes=[("tB", 0)], sem=sV, inc=1,
           f=lambda e: e.tensor_scalar_max(out=tB[:, :], in0=lamb[:, :], scalar1=0.0))
    p.step("vector", reads=[("tA", 0), ("tB", 0)], writes=[("tB", 0)], sem=sV, inc=1,
           f=lambda e: e.tensor_add(out=tB[:, :], in0=tB[:, :], in1=tA[:, :]))
    p.step("vector", reads=[("tB", 0)], writes=[("sdm", 0)], sem=sV, inc=1,
           f=lambda e: e.tensor_scalar_mul(out=sdm[:, :], in0=tB[:, :], scalar1=8.0))
    p.step("vector", reads=[("tB", 0)], writes=[("sdmh", 0)], sem=sV, inc=1,
           f=lambda e: e.tensor_scalar_mul(out=sdmh[:, :], in0=tB[:, :], scalar1=4.0))
    p.step("vector", reads=[("bgb", 0)], writes=[("bgbh", 0)], sem=sV, inc=1,
           f=lambda e: e.tensor_scalar_mul(out=bgbh[:, :], in0=bgb[:, :], scalar1=0.5))
    p.step("vector", reads=[("binb", 0)], writes=[("binbh", 0)], sem=sV, inc=1,
           f=lambda e: e.tensor_scalar_mul(out=binbh[:, :], in0=binb[:, :], scalar1=0.5))

    def ln_group(g, src_ap, src_fam, with_router=True, resident=False):
        tts = [g * GT + ti for ti in range(GT)]
        if not resident:
            for tt in tts:
                ti = tt % GT
                dma("sync", dX[ti], xg[:, ti, :], src_ap[tt * 128:tt * 128 + 128, :],
                    reads=([(src_fam, tt)] if src_fam else []), writes=[("xg", ti)])
        for tt in tts:
            ti = tt % GT
            ys = ti % 2
            kxg = ("xg", ti)
            kdV = ("ydmp", ys)
            p.step("vector", reads=[kxg], writes=[kdV, ("st", ti, 0)], sem=sV, inc=1,
                   f=lambda e, ti=ti, ys=ys: e.scalar_tensor_tensor(
                       out=ydmp[:, ys, :], in0=xg[:, ti, :], scalar=1.0 / H,
                       in1=xg[:, ti, :], op0=ALU.mult, op1=ALU.mult,
                       accum_out=stats[:, ti, 0:1]))
        for tt in tts:
            ti = tt % GT
            ys = ti % 2
            kxg = ("xg", ti)
            kdS = ("ydmp", 2 + ys)
            p.step("scalar", reads=[kxg], writes=[kdS, ("st", ti, 1)], sem=sS, inc=1,
                   f=lambda e, ti=ti, ys=ys: e.activation(
                       out=ydmp[:, 2 + ys, :], in_=xg[:, ti, :], func=AF.Copy,
                       scale=1.0 / H, accum_out=stats[:, ti, 1:2]))
        for tt in tts:
            ti = tt % GT
            p.step("vector", reads=[("st", ti, 1)], writes=[("st", ti, 2)], sem=sV, inc=1,
                   f=lambda e, ti=ti: e.tensor_scalar(
                       out=stats[:, ti, 2:3], in0=stats[:, ti, 1:2],
                       scalar1=stats[:, ti, 1:2], scalar2=-1.0,
                       op0=ALU.mult, op1=ALU.mult))
            p.step("vector", reads=[("st", ti, 0), ("st", ti, 2)], writes=[("st", ti, 3)],
                   sem=sV, inc=1,
                   f=lambda e, ti=ti: e.scalar_tensor_tensor(
                       out=stats[:, ti, 3:4], in0=stats[:, ti, 2:3], scalar=EPS_LN,
                       in1=stats[:, ti, 0:1], op0=ALU.add, op1=ALU.add))
        for tt in tts:
            ti = tt % GT
            p.step("scalar", reads=[("st", ti, 3)], writes=[("st", ti, 4)], sem=sS, inc=1,
                   f=lambda e, ti=ti: e.activation(out=stats[:, ti, 4:5],
                                                   in_=stats[:, ti, 3:4], func=AF.Sqrt))
        for tt in tts:
            ti = tt % GT
            p.step("vector", reads=[("st", ti, 4)], writes=[("st", ti, 5)], sem=sV, inc=1,
                   f=lambda e, ti=ti: e.reciprocal(out=stats[:, ti, 5:6],
                                                   in_=stats[:, ti, 4:5]))
        for tt in tts:
            ti = tt % GT
            ys = ti % 2
            kxg = ("xg", ti)
            for hh in (0, 1):
                lo = hh * 1024
                pt = ps_c if hh == 0 else ps_d
                pn = "c" if hh == 0 else "d"
                pslots = [("psum", pn, 0), ("psum", pn, 1)]
                p.step("vector", reads=[kxg, ("st", ti, 1), ("st", ti, 5)],
                       writes=[("yf", ys, hh)], sem=sV, inc=1,
                       f=lambda e, ti=ti, ys=ys, lo=lo: e.tensor_scalar(
                           out=yf[:, ys, lo:lo + 1024], in0=xg[:, ti, lo:lo + 1024],
                           scalar1=stats[:, ti, 1:2], scalar2=stats[:, ti, 5:6],
                           op0=ALU.subtract, op1=ALU.mult))

                def tr_f(e, ys=ys, lo=lo, pt=pt):
                    last = None
                    for kk in range(8):
                        last = e.transpose(pt[:, kk * 128:kk * 128 + 128],
                                           yf[:, ys, lo + kk * 128:lo + kk * 128 + 128],
                                           idf[:, :])
                    return last
                p.step("tensor", reads=[("yf", ys, hh), ("idf", 0)], writes=pslots,
                       sem=sT, inc=1, f=tr_f)
                p.step("scalar", reads=pslots, writes=[("xlnT", ti)], sem=sS, inc=1,
                       f=lambda e, ti=ti, hh=hh, pt=pt: e.copy(
                           out=xlnT[:, hh * 8:hh * 8 + 8, ti * 128:ti * 128 + 128],
                           in_=pt[:, 0:1024].rearrange("p (k j) -> p k j", j=128)))
            if with_router:
                router_tile(tt)

    def router_tile(tt):
        ti = tt % GT
        pn = "a" if ti % 2 == 0 else "b"
        pst = PS[pn]

        def rt_f(e, ti=ti, pst=pst):
            last = None
            for k in range(KH):
                last = e.matmul(pst[:, 0:E], xlnT[:, k, ti * 128:ti * 128 + 128],
                                gwb[:, k, :], start=(k == 0), stop=(k == KH - 1))
            return last
        p.step("tensor", reads=[("xlnT", ti), ("gwb", 0)], writes=[("psum", pn, 0)],
               sem=sT, inc=1, f=rt_f)
        p.step("scalar", reads=[("psum", pn, 0)], writes=[("rtr", 0)], sem=sS, inc=1,
               f=lambda e, pst=pst: e.activation(out=rtr[:, 0, 0:E], in_=pst[:, 0:E],
                                                 func=AF.Sigmoid))
        p.step("vector", reads=[("rtr", 0), ("ebs", 0)], writes=[("rtr", 1)], sem=sV, inc=1,
               f=lambda e: e.tensor_add(out=rtr[:, 1, 0:E], in0=rtr[:, 0, 0:E], in1=ebs[:, :]))
        p.step("vector", reads=[("rtr", 1)], writes=[("rtr", 2)], sem=sV, inc=1,
               f=lambda e: e.tensor_copy(out=rtr[:, 2, 0:E], in_=rtr[:, 1, 0:E]))
        for j in range(TOPK - 1):
            p.step("vector", reads=[("rtr", 1)], writes=[("rst", 0)], sem=sV, inc=1,
                   f=lambda e: e.reduce_max(out=rst[:, 0:1], in_=rtr[:, 1, 0:E], axis=AX.X))
            p.step("vector", reads=[("rtr", 1), ("rst", 0)], writes=[("rtr", 3)], sem=sV, inc=1,
                   f=lambda e: e.tensor_scalar(out=rtr[:, 3, 0:E], in0=rtr[:, 1, 0:E],
                                               scalar1=rst[:, 0:1], scalar2=None,
                                               op0=ALU.is_equal))
            p.step("vector", reads=[("rtr", 1), ("rtr", 3)], writes=[("rtr", 1)], sem=sV, inc=1,
                   f=lambda e: e.scalar_tensor_tensor(out=rtr[:, 1, 0:E], in0=rtr[:, 3, 0:E],
                                                      scalar=-1e30, in1=rtr[:, 1, 0:E],
                                                      op0=ALU.mult, op1=ALU.add))
        p.step("vector", reads=[("rtr", 1)], writes=[("rst", 0)], sem=sV, inc=1,
               f=lambda e: e.reduce_max(out=rst[:, 0:1], in_=rtr[:, 1, 0:E], axis=AX.X))
        p.step("vector", reads=[("rtr", 2), ("rst", 0)], writes=[("rtr", 1)], sem=sV, inc=1,
               f=lambda e: e.tensor_scalar(out=rtr[:, 1, 0:E], in0=rtr[:, 2, 0:E],
                                           scalar1=rst[:, 0:1], scalar2=None, op0=ALU.is_ge))
        p.step("vector", reads=[("rtr", 0), ("rtr", 1)], writes=[("rtr", 0)], sem=sV, inc=1,
               f=lambda e: e.tensor_mul(out=rtr[:, 0, 0:E], in0=rtr[:, 0, 0:E],
                                        in1=rtr[:, 1, 0:E]))
        p.step("vector", reads=[("rtr", 0)], writes=[("rst", 1)], sem=sV, inc=1,
               f=lambda e: e.reduce_sum(out=rst[:, 1:2], in_=rtr[:, 0, 0:E], axis=AX.X))
        p.step("vector", reads=[("rst", 1)], writes=[("rst", 2)], sem=sV, inc=1,
               f=lambda e: e.reciprocal(out=rst[:, 2:3], in_=rst[:, 1:2]))
        p.step("vector", reads=[("rtr", 0), ("rst", 2)], writes=[("c", ti)], sem=sV, inc=1,
               f=lambda e, ti=ti: e.tensor_scalar(out=cbuf[:, ti, :], in0=rtr[:, 0, 0:E],
                                                  scalar1=rst[:, 2:3], scalar2=0.5,
                                                  op0=ALU.mult, op1=ALU.mult))

    blk = [0]
    stgi = [0]
    csel = [0]
    qsel = [0]

    def cast_step(eng, dst_ap, src_ap, reads, writes):
        sem = {"vector": sV, "scalar": sS}[eng]
        if eng == "scalar":
            p.step(eng, reads=reads, writes=writes, sem=sem, inc=1,
                   f=lambda e, d=dst_ap, s=src_ap: e.copy(out=d, in_=s))
        else:
            p.step(eng, reads=reads, writes=writes, sem=sem, inc=1,
                   f=lambda e, d=dst_ap, s=src_ap: e.tensor_copy(out=d, in_=s))

    def stage_load(src_f32, gur_slot, key_slot, cache_ap, cache_sem, ckey):
        slot = stgi[0] % 2
        stgi[0] += 1
        for h2 in (0, 1):
            lo = h2 * (KH // 2)
            kst = ("stg", slot, h2)
            dma(("sync", "scalar")[h2], dStg[slot * 2 + h2],
                stg[:, slot, lo * 128:(lo + KH // 2) * 128],
                src_f32[:, lo:lo + KH // 2, :].rearrange("p k j -> p (k j)"),
                writes=[kst])
            cast_step("scalar",
                      gur[:, gur_slot, lo:lo + KH // 2, :],
                      stg[:, slot, lo * 128:(lo + KH // 2) * 128]
                      .rearrange("p (k j) -> p k j", j=128),
                      reads=[kst], writes=[("gur", key_slot, h2)])
        dma("gpsimd", cache_sem, cache_ap, gur[:, gur_slot, :, :],
            reads=[("gur", key_slot, 0), ("gur", key_slot, 1)], writes=[ckey])

    def stream_pair(pair_src, pair_cache, ckbase, pass0):
        slot = blk[0] % NSLOT
        assert slot % 2 == 0
        blk[0] += 2
        ck0, ck1 = ckbase + (0,), ckbase + (1,)
        if pass0:
            for w in (0, 1):
                stage_load(pair_src[w], slot + w, slot + w,
                           pair_cache[w], dCw[slot + w], (ck0, ck1)[w])
        else:
            dma("sync", dGu[slot], gur[:, slot, :, :],
                pair_cache[0], reads=[ck0], writes=[("gur", slot, 0), ("gur", slot, 1)])
            dma("scalar", dGu[slot + 1], gur[:, slot + 1, :, :],
                pair_cache[1], reads=[ck1],
                writes=[("gur", slot + 1, 0), ("gur", slot + 1, 1)])
        return slot, slot + 1

    wsti = [0]

    def stream_wd_chunks(e_idx, k2):
        for k in (k2, k2 + 1):
            slot = wsti[0] % 2
            wsti[0] += 1
            kst = ("wdstg", slot)
            dma("gpsimd", dStgW[slot], wdstg[:, slot, :], wdd[e_idx, k], writes=[kst])
            cast_step("vector", wdsb[:, k, :], wdstg[:, slot, :],
                      reads=[kst], writes=[("wd", k)])
            dma("gpsimd", dCwd[k], cwd[e_idx, k], wdsb[:, k, :],
                reads=[("wd", k)], writes=[("cwd", e_idx, k)])

    XLK = [("xlnT", t2) for t2 in range(GT)]

    def mm_dmaj(slot, pn):
        ps_t = PS[pn]

        def f(e, slot=slot, ps_t=ps_t):
            last = None
            for k in range(KH):
                for (s0, w) in TCH:
                    last = e.matmul(ps_t[:, s0:s0 + w], gur[:, slot, k, :],
                                    xlnT[:, k, s0:s0 + w],
                                    start=(k == 0), stop=(k == KH - 1))
            return last
        p.step("tensor", reads=[("gur", slot, 0), ("gur", slot, 1)] + XLK,
               writes=[("psum", pn, s0 // 512) for (s0, _) in TCH],
               sem=sT, inc=1, f=f)

    def ffn_call(src_ap, src_fam, dst_ap, dst_fam, pass0_call, resident_g0=False):
        pref = {}
        for g in range(NG):
            pass0 = pass0_call and g == 0
            ln_group(g, src_ap, src_fam, resident=(resident_g0 and g == 0))
            for e_idx in range(E + 1):
                if not pass0:
                    for j, k2 in enumerate(range(0, KI, 2)):
                        dma("gpsimd", dWd[k2], wdsb[:, k2:k2 + 2, :],
                            cwd[e_idx, k2:k2 + 2].rearrange("w p n -> p w n"),
                            reads=[("cwd", e_idx, k2), ("cwd", e_idx, k2 + 1)],
                            writes=[("wd", k2), ("wd", k2 + 1)])
                def act_m(m):
                    pg, pu = ("a", "b") if m % 2 == 0 else ("c", "d")
                    ss = m % 2
                    for hh in (0, 1):
                        lo = hh * 512
                        p.step("scalar", reads=[("psum", pg, hh)],
                               writes=[("sgh", ss, hh)], sem=sS, inc=1,
                               f=lambda e, ss=ss, pg=pg, lo=lo: e.activation(
                                   out=sg[:, ss, lo:lo + 512],
                                   in_=PS[pg][:, lo:lo + 512], func=AF.Silu))
                        p.step("vector",
                               reads=[("sgh", ss, hh), ("psum", pu, hh)],
                               writes=[("hp", m, hh)], sem=sV, inc=1,
                               f=lambda e, ss=ss, m=m, pu=pu, lo=lo: e.tensor_mul(
                                   out=hp[:, m, lo:lo + 512],
                                   in0=sg[:, ss, lo:lo + 512],
                                   in1=PS[pu][:, lo:lo + 512]))
                for m in range(MI):
                    if (g, e_idx, m) in pref:
                        gslot, uslot = pref.pop((g, e_idx, m))
                    else:
                        gslot, uslot = stream_pair(wgu[e_idx, m], cgu[e_idx, m],
                                                   ("cgu", e_idx, m), pass0)
                    if m >= 1:
                        act_m(m - 1)
                    pg, pu = ("a", "b") if m % 2 == 0 else ("c", "d")
                    mm_dmaj(gslot, pg)
                    mm_dmaj(uslot, pu)
                    if pass0 and 1 <= m <= 4:
                        stream_wd_chunks(e_idx, (m - 1) * 2)
                act_m(MI - 1)
                nxt = (g, e_idx + 1) if e_idx < E else \
                      ((g + 1, 0) if g + 1 < NG else None)
                if nxt is not None:
                    ng, ne = nxt
                    np0 = pass0_call and ng == 0
                    for m2 in (0, 1):
                        pref[(ng, ne, m2)] = stream_pair(
                            wgu[ne, m2], cgu[ne, m2], ("cgu", ne, m2), np0)
                for ti in range(GT):
                    pr = ("a", "b") if ti % 2 == 0 else ("c", "d")
                    q0, q1 = PS[pr[0]], PS[pr[1]]

                    def ph2_f(e, ti=ti, q0=q0, q1=q1):
                        last = None
                        for k2 in range(KI):
                            for (s0, w) in HCH:
                                dst = q0 if s0 < 1024 else q1
                                ds0 = s0 % 1024
                                last = e.matmul(dst[:, ds0:ds0 + w],
                                                hp[:, k2, ti * 128:ti * 128 + 128],
                                                wdsb[:, k2, s0:s0 + w],
                                                start=(k2 == 0), stop=(k2 == KI - 1))
                        return last
                    dslots = sorted({("psum", pr[0] if s0 < 1024 else pr[1],
                                      (s0 % 1024) // 512) for (s0, _) in HCH})
                    p.step("tensor",
                           reads=[("hp", k2, ti // 4) for k2 in range(KI)]
                           + [("wd", k2) for k2 in range(KI)],
                           writes=dslots, sem=sT, inc=1, f=ph2_f)
                    for qi, qn in enumerate(pr):
                        lo = qi * 1024
                        if lo >= H:
                            break
                        w = min(1024, H - lo)
                        rd = [("psum", qn, 0)] + ([("psum", qn, 1)] if w > 512 else [])
                        if e_idx < E:
                            rd.append(("c", ti))
                        p.step("vector", reads=rd, writes=[("xg", ti)], sem=sV, inc=1,
                               f=lambda e, q=PS[qn], lo=lo, w=w, ti=ti, ee=e_idx:
                               e.scalar_tensor_tensor(
                                   out=xg[:, ti, lo:lo + w], in0=q[:, 0:w],
                                   scalar=(cbuf[:, ti, ee:ee + 1] if ee < E else 0.5),
                                   in1=xg[:, ti, lo:lo + w], op0=ALU.mult, op1=ALU.add))
            if pass0:
                for k2 in range(KI):
                    p.join_keys([("cwd", e2, k2) for e2 in range(E + 1)], dCwd[k2])
            for ti in range(GT):
                tt = g * GT + ti
                dma("scalar", dXW[ti], dst_ap[tt * 128:tt * 128 + 128, :], xg[:, ti, :],
                    reads=[("xg", ti)], writes=[(dst_fam, tt)])

    def griffin():
        p.step("vector", writes=[("zb", 0)], sem=sV, inc=1,
               f=lambda e: e.memset(zb[:, 0:TG], 0.0))

        def chain_m(m, g):
            r = m % 2
            pa, pb = ("a", "b") if m % 2 == 0 else ("c", "d")
            kpa = [("psum", pa, 0), ("psum", pa, 1)]
            kpb = [("psum", pb, 0), ("psum", pb, 1)]
            al, xb_ = ("gfa", r), ("gfa", 2 + r)
            hh, AA = ("gfa", 4 + r), ("gfa", 6 + r)
            p.step("scalar", reads=kpb + [("bgbh", 0)], writes=[("gsc", r)],
                   sem=sS, inc=1,
                   f=lambda e, r=r, m=m, pb=pb: e.activation(
                       out=gsc[:, r, :], in_=PS[pb][:, 0:TG],
                       func=AF.Tanh, scale=0.5, bias=bgbh[:, m:m + 1]))
            p.step("scalar", reads=kpa + [("binbh", 0)], writes=[("sg", r)],
                   sem=sS, inc=1,
                   f=lambda e, r=r, m=m, pa=pa: e.activation(
                       out=sg[:, r, :], in_=PS[pa][:, 0:TG],
                       func=AF.Tanh, scale=0.5, bias=binbh[:, m:m + 1]))
            p.step("vector", reads=[("gsc", r), ("sdmh", 0)], writes=[("gsc", 2 + r)],
                   sem=sV, inc=1,
                   f=lambda e, r=r, m=m: e.tensor_scalar(
                       out=gsc[:, 2 + r, :], in0=gsc[:, r, :],
                       scalar1=sdmh[:, m:m + 1], scalar2=sdmh[:, m:m + 1],
                       op0=ALU.mult, op1=ALU.add))
            p.step("scalar", reads=[("gsc", 2 + r)], writes=[al], sem=sS, inc=1,
                   f=lambda e, r=r: e.activation(out=gfa[:, r, :], in_=gsc[:, 2 + r, :],
                                                 func=AF.Exp, scale=-1.0))
            p.step("vector", reads=[al], writes=[("gsc", 2 + r)], sem=sV, inc=1,
                   f=lambda e, r=r: e.tensor_mul(out=gsc[:, 2 + r, :],
                                                 in0=gfa[:, r, :], in1=gfa[:, r, :]))
            p.step("scalar", reads=[("gsc", 2 + r), ("onec", 0)],
                   writes=[("gsc", 2 + r)], sem=sS, inc=1,
                   f=lambda e, r=r: e.activation(out=gsc[:, 2 + r, :],
                                                 in_=gsc[:, 2 + r, :], func=AF.Ln,
                                                 scale=-1.0, bias=onec[:, 0:1]))
            p.step("scalar", reads=[("gsc", 2 + r)], writes=[("gsc", 2 + r)],
                   sem=sS, inc=1,
                   f=lambda e, r=r: e.activation(out=gsc[:, 2 + r, :],
                                                 in_=gsc[:, 2 + r, :], func=AF.Exp,
                                                 scale=0.5))
            p.step("vector", reads=[("sg", r)], writes=[("sg", r)], sem=sV, inc=1,
                   f=lambda e, r=r: e.tensor_scalar(
                       out=sg[:, r, :], in0=sg[:, r, :], scalar1=0.5, scalar2=0.5,
                       op0=ALU.mult, op1=ALU.add))
            p.step("vector", reads=[("sg", r)] + XLK, writes=[xb_], sem=sV, inc=1,
                   f=lambda e, r=r, m=m: e.tensor_mul(out=gfa[:, 2 + r, :],
                                                      in0=sg[:, r, :],
                                                      in1=xlnT[:, m, :]))
            p.step("vector", reads=[xb_, ("gsc", 2 + r)], writes=[xb_], sem=sV, inc=1,
                   f=lambda e, r=r: e.tensor_mul(out=gfa[:, 2 + r, :],
                                                 in0=gfa[:, 2 + r, :],
                                                 in1=gsc[:, 2 + r, :]))
            p.step("vector", reads=[al, xb_] + ([("hend", m)] if g else []),
                   writes=[hh], sem=sV, inc=1,
                   f=lambda e, r=r, m=m, g=g: e.tensor_tensor_scan(
                       out=gfa[:, 4 + r, :], data0=gfa[:, r, :],
                       data1=gfa[:, 2 + r, :],
                       initial=(0.0 if g == 0 else hend[:, m:m + 1]),
                       op0=ALU.mult, op1=ALU.add))
            p.step("vector", reads=[al, ("zb", 0)] + ([("aend", m)] if g else []),
                   writes=[AA], sem=sV, inc=1,
                   f=lambda e, r=r, m=m, g=g: e.tensor_tensor_scan(
                       out=gfa[:, 6 + r, :], data0=gfa[:, r, :], data1=zb[:, 0:TG],
                       initial=(1.0 if g == 0 else aend[:, m:m + 1]),
                       op0=ALU.mult, op1=ALU.add))
            p.step("vector", reads=[hh], writes=[("hend", m)], sem=sV, inc=1,
                   f=lambda e, r=r, m=m: e.tensor_copy(out=hend[:, m:m + 1],
                                                       in_=gfa[:, 4 + r, TG - 1:TG]))
            p.step("vector", reads=[AA], writes=[("aend", m)], sem=sV, inc=1,
                   f=lambda e, r=r, m=m: e.tensor_copy(out=aend[:, m:m + 1],
                                                       in_=gfa[:, 6 + r, TG - 1:TG]))
            dma("gpsimd", dHdH[r], hdram[m, :, g * TG:(g + 1) * TG], gfa[:, 4 + r, :],
                reads=[hh], writes=[("hd", m, g)])
            dma("scalar", dHdA[r], adram[m, :, g * TG:(g + 1) * TG], gfa[:, 6 + r, :],
                reads=[AA], writes=[("ad", m, g)])

        for g in range(NG):
            pass0 = g == 0
            ln_group(g, xr, "xr", with_router=False)
            prev = None
            for m in range(KH):
                islot, gslot = stream_pair(wio[m], cio[m], ("cio", m), pass0)
                if prev is not None:
                    chain_m(prev, g)
                pa, pb = ("a", "b") if m % 2 == 0 else ("c", "d")
                mm_dmaj(islot, pa)
                mm_dmaj(gslot, pb)
                prev = m
            chain_m(KH - 1, g)
        for r in range(2):
            p.join_keys([("hd", m, g) for m in range(KH) for g in range(NG) if m % 2 == r],
                        dHdH[r])
            p.join_keys([("ad", m, g) for m in range(KH) for g in range(NG) if m % 2 == r],
                        dHdA[r])

        def xchg_f(e):
            with e.register("hreg") as hreg, e.register("cur") as cur, \
                 e.register("tgt") as tgt, e.register("cnt") as cnt, \
                 e.register("go") as go:
                e.reg_load(hreg, hsb[0:1, 0:1])
                myslot = e.snap(hreg, min_val=0, max_val=1)
                e.dma_start(out=shd[bass.ds(myslot, 1), :, :],
                            in_=hend[:, :]).then_inc(dPoll, 16)
                e.wait_ge(dPoll, 16)
                e.dma_start(out=shf[bass.ds(myslot, 1), :, :],
                            in_=nsb[:, 0:1]).then_inc(dPoll, 16)
                e.wait_ge(dPoll, 32)
                e.reg_load(tgt, nsb[0:1, 0:1])
                e.reg_mov(cnt, 32)
                e.reg_mov(go, 1)
                with e.While(go):
                    e.dma_start(out=fsb[:, 0:1],
                                in_=shf[0:1, :, :]).then_inc(dPoll, 16)
                    e.reg_add(cnt, cnt, 16)
                    e.wait_ge(dPoll, cnt)
                    e.reg_load(cur, fsb[0:1, 0:1])
                    e.reg_alu(go, cur, tgt, ALU.not_equal)
                return e.dma_start(out=ccrs[:, :], in_=shd[0:1, :, :])
        for m in (0, 1):
            r = m % 2
            dma("sync", dHb[r], gfb[:, r, :], hdram[m, :, TG:2 * TG],
                reads=[("hd", m, 1)], writes=[("gfb", r)])
            dma("scalar", dAb[r], gfb[:, 2 + r, :], adram[m, :, TG:2 * TG],
                reads=[("ad", m, 1)], writes=[("gfb", 2 + r)])
        p.step("gpsimd", reads=[("hend", m) for m in range(KH)]
               + [("nsb", 0), ("hsb", 0)],
               writes=[("ccrs", 0)], sem=dCc, inc=16, f=xchg_f)
        p.step("vector", reads=[("ccrs", 0), ("flg", 0)], writes=[("ceff", 0)], sem=sV, inc=1,
               f=lambda e: e.tensor_scalar_mul(out=ceff[:, :], in0=ccrs[:, :],
                                               scalar1=flg[:, 0:1]))

        for g in (1, 0):
            if g == 0:
                for ti in range(GT):
                    tt = g * GT + ti
                    dma("gpsimd", dX[ti], xg[:, ti, :], xr[tt * 128:tt * 128 + 128, :],
                        reads=[("xr", tt)], writes=[("xg", ti)])
            for m in range(KH):
                r = m % 2
                kh_, ka = ("gfb", r), ("gfb", 2 + r)
                if not (g == 1 and m < 2):
                    dma("sync", dHb[r], gfb[:, r, :], hdram[m, :, g * TG:(g + 1) * TG],
                        reads=[("hd", m, g)], writes=[kh_])
                    dma("scalar", dAb[r], gfb[:, 2 + r, :],
                        adram[m, :, g * TG:(g + 1) * TG],
                        reads=[("ad", m, g)], writes=[ka])
                kht = ("ght", r)
                p.step("vector", reads=[kh_, ka, ("ceff", 0)], writes=[kht], sem=sV, inc=1,
                       f=lambda e, r=r, m=m: e.scalar_tensor_tensor(
                           out=ght[:, r, :], in0=gfb[:, 2 + r, :], scalar=ceff[:, m:m + 1],
                           in1=gfb[:, r, :], op0=ALU.mult, op1=ALU.add))
                pn = "c" if r == 0 else "d"
                pst = PS[pn]
                psk = [("psum", pn, 0), ("psum", pn, 1)]

                def htr_f(e, r=r, pst=pst):
                    last = None
                    for t2 in range(GT):
                        last = e.transpose(pst[:, t2 * 128:t2 * 128 + 128],
                                           ght[:, r, t2 * 128:t2 * 128 + 128], idf[:, :])
                    return last
                p.step("tensor", reads=[kht, ("idf", 0)], writes=psk, sem=sT, inc=1, f=htr_f)
                xgk = [("xg", t2) for t2 in range(GT)]
                p.step("vector", reads=psk + xgk, writes=xgk, sem=sV, inc=1,
                       f=lambda e, m=m, pst=pst: e.tensor_add(
                           out=xg[:, :, m * 128:m * 128 + 128],
                           in0=xg[:, :, m * 128:m * 128 + 128],
                           in1=pst[:, 0:GT * 128].rearrange("p (t j) -> p t j", j=128)))
            for ti in range(GT):
                tt = g * GT + ti
                dma("scalar", dXW[ti], xr[tt * 128:tt * 128 + 128, :], xg[:, ti, :],
                    reads=[("xg", ti)], writes=[("xr", tt)])

    ffn_call(x_in, None, xr, "xr", pass0_call=True)
    griffin()
    ffn_call(xr, "xr", y_out, "yd", pass0_call=False, resident_g0=True)

    for t in range(GT):
        p.wait_abs("gpsimd", dXW[t], dXW[t].n)

    with st:
        block = nc.Block()
        with block:
            p.emit(block)
    return nc



_CACHE = {}
_NONCE = [int(np.random.SeedSequence().entropy % 2000000000) + 1]


def kernel(L0_attn_x, L0_attn_W_in, L0_attn_b_in, L0_attn_W_gate, L0_attn_b_gate,
           L0_attn_forget_lambda, L0_ffn_gate_w, L0_ffn_expert_bias, L0_ffn_w_gate,
           L0_ffn_w_up, L0_ffn_w_down, L0_ffn_sw_gate, L0_ffn_sw_up, L0_ffn_sw_down):
    cfg = derived(full_cfg())
    H, E, KH, TLOC = cfg["H"], cfg["E"], cfg["KH"], cfg["TLOC"]
    B = 4

    x = np.ascontiguousarray(np.asarray(L0_attn_x, dtype=np.float32))
    arr_gu, arr_wd, arr_io, arr_gw = pack_weights(
        cfg,
        np.asarray(L0_ffn_w_gate, np.float32), np.asarray(L0_ffn_w_up, np.float32),
        np.asarray(L0_ffn_w_down, np.float32), np.asarray(L0_ffn_sw_gate, np.float32),
        np.asarray(L0_ffn_sw_up, np.float32), np.asarray(L0_ffn_sw_down, np.float32),
        np.asarray(L0_attn_W_in, np.float32), np.asarray(L0_attn_W_gate, np.float32),
        np.asarray(L0_ffn_gate_w, np.float32))
    eb = np.ascontiguousarray(np.broadcast_to(
        np.asarray(L0_ffn_expert_bias, np.float32), (128, E)))
    bin_a = np.ascontiguousarray(np.asarray(L0_attn_b_in, np.float32).reshape(KH, 128).T)
    bg_a = np.ascontiguousarray(np.asarray(L0_attn_b_gate, np.float32).reshape(KH, 128).T)
    lam_a = np.ascontiguousarray(
        np.asarray(L0_attn_forget_lambda, np.float32).reshape(KH, 128).T)
    identf = np.eye(128, dtype=np.float32)

    if "nc" not in _CACHE:
        _CACHE["nc"] = build_program(full_cfg())
    nc = _CACHE["nc"]
    _NONCE[0] = (_NONCE[0] * 1103515245 + 12345) % 2147483629 + 1

    xs = x.reshape(B, 2, TLOC, H)
    in_maps = []
    for c in range(8):
        b, half = c // 2, c % 2
        in_maps.append(dict(
            x=np.ascontiguousarray(xs[b, half]), wgu=arr_gu, wd=arr_wd, wio=arr_io,
            gw=arr_gw, ebias=eb, bin=bin_a, bgate=bg_a, lam=lam_a,
            flag=np.full((128, 1), float(half), np.float32),
            nflag=np.full((128, 1), 1.0 - float(half), np.float32),
            identf=identf,
            onec=np.full((128, 1), 1.0 + 1e-6, np.float32),
            nonce=np.array([[_NONCE[0]]], np.int32),
            halfi=np.array([[half]], np.int32)))
    res = run_bass_kernel_spmd(nc, in_maps, core_ids=list(range(8)))
    out = np.empty((B, 4096, H), np.float32)
    for c in range(8):
        b, half = c // 2, c % 2
        out[b, half * TLOC:(half + 1) * TLOC, :] = res.results[c]["y"]
    return out
